# version 14
# baseline (speedup 1.0000x reference)
import math
import numpy as np
import concourse.bass as bass
import concourse.mybir as mybir
from concourse.library_config import mlp as MLP_LIB

F32 = mybir.dt.float32
BF16 = mybir.dt.bfloat16
I16 = mybir.dt.int16
U32 = mybir.dt.uint32
U8 = mybir.dt.uint8
AF = mybir.ActivationFunctionType
ALU = mybir.AluOpType
AX = mybir.AxisListType

MASK_NEG = -30000.0
EXP_SHIFT = 4.0
INV_SQRT_HD = float(1.0 / np.sqrt(128.0))

S, D = 2048, 2048
NH = 16
D_CROSS, D_PRIV = 8192, 1024
N_EXPERTS = 16384
nT = S // 128
nDC = D // 128
nCC = D_CROSS // 128
nPC = D_PRIV // 128
CG = 4
GCALL = 256
SLOTC = 2
NGB = 5
nGC = 64 // SLOTC
TROW = D_PRIV + D


def host_consts():
    import ml_dtypes
    eye = np.eye(128, dtype=np.float32)
    ckq = np.where(np.arange(128)[None, :] >= np.arange(128)[:, None],
                   0.0, MASK_NEG).astype(np.float32)
    cqk = np.where(np.arange(128)[None, :] <= np.arange(128)[:, None],
                   0.0, MASK_NEG).astype(np.float32)
    fold = np.zeros((8, 128, 32), np.float32)
    for u in range(8):
        for p in range(32):
            fold[u, 16 * u + (p % 16), p] = 1.0
    return {
        "negs": np.full((128, 1), -EXP_SHIFT, np.float32),
        "foldc": fold,
        "eyef": eye,
        "eyeb": eye.astype(ml_dtypes.bfloat16),
        "caus_kq": ckq,
        "caus_qk": cqk,
        "onesf": np.ones((1, 128), np.float32),
        "onesb": np.ones((1, 128), np.float32).astype(ml_dtypes.bfloat16),
        "onesc": np.ones((128, 1), np.float32).astype(ml_dtypes.bfloat16),
    }


class Sem:
    def __init__(self, h):
        self.h = h
        self.n = 0


class MK:
    ENGINES = ("sync", "gpsimd", "tensor", "vector", "scalar")

    def __init__(self, stage=5, debug_outputs=()):
        self.stage = stage
        self.dbgnames = set(debug_outputs)
        self.nc = bass.Bass("TRN2", target_bir_lowering=False)
        self.ops = {e: [] for e in self.ENGINES}
        self.semcms = []
        self.build()

    def ins(self, engine, fn, sem=None, k=1):
        if sem is None:
            self.ops[engine].append(fn)
            return None
        sem.n += k
        self.ops[engine].append(
            lambda eng, fn=fn, s=sem, k=k: fn(eng).then_inc(s.h, k))
        return sem.n

    def wait(self, engine, sem, val=None):
        v = sem.n if val is None else val
        if v > 0:
            self.ops[engine].append(lambda eng, s=sem, v=v: eng.wait_ge(s.h, v))

    def waitt(self, engine, tup):
        s, v = tup
        if s is not None and v > 0:
            self.wait(engine, s, v)

    def mksem(self, name):
        cm = self.nc.semaphore(name)
        h = cm.__enter__()
        self.semcms.append(cm)
        return Sem(h)

    def ld(self, dst, src, sem=None, eng="sync"):
        sem = sem or self.s_x
        return self.ins(eng, lambda e, d=dst, s=src: e.dma_start(out=d, in_=s),
                        sem, 16)

    def st(self, dst, src, sem=None, eng="scalar"):
        sem = sem or self.s_out
        return self.ins(eng, lambda e, d=dst, s=src: e.dma_start(out=d, in_=s),
                        sem, 16)

    def ldT(self, dst, src, sem=None, eng="sync"):
        sem = sem or self.s_x
        return self.ins(eng, lambda e, d=dst, s=src: e.dma_start(
            out=d, in_=s, transpose=True), sem, 16)

    def bank(self, i, off=0, w=512):
        return self.psum[:, 512 * i + off:512 * i + off + w]

    def bwait(self, engine, i):
        self.waitt(engine, self.bank_free[i])

    def bset(self, i, sem):
        self.bank_free[i] = (sem, sem.n)

    def transpose_evac(self, src_ap, dst_ap, bank_i, ident, eng="vector"):
        p = src_ap.shape[0]
        f = src_ap.shape[-1]
        idt = ident if p == 128 else ident[0:p, 0:p]
        sem = self.s_dve if eng == "vector" else self.s_act
        self.bwait("tensor", bank_i)
        self.ins("tensor", lambda t, s=src_ap, b=bank_i, f=f, p=p, idt=idt:
                 t.transpose(out=self.bank(b, 0, p)[0:f, :], in_=s,
                             identity=idt), self.s_mm)
        mmv = self.s_mm.n
        self.wait(eng, self.s_mm, mmv)
        if eng == "vector":
            self.ins(eng, lambda v, d=dst_ap, b=bank_i, f=f, p=p:
                     v.tensor_copy(out=d, in_=self.bank(b, 0, p)[0:f, :]),
                     sem)
        else:
            self.ins(eng, lambda sc, d=dst_ap, b=bank_i, f=f, p=p:
                     sc.copy(out=d, in_=self.bank(b, 0, p)[0:f, :]), sem)
        self.bset(bank_i, sem)
        return sem.n

    def build(self):
        nc = self.nc
        st = self.stage
        ins, wait, ld, ldT = self.ins, self.wait, self.ld, self.ldT

        def di(name, shape, dt=F32):
            return nc.dram_tensor(name, shape, dt, kind="ExternalInput")
        X = di("x", [S, D])
        Wq = di("wq", [D, D]); Bq = di("bq", [D])
        Wk = di("wk", [D, D]); Bk = di("bk", [D])
        Wv = di("wv", [D, D]); Bv = di("bv", [D])
        Wo = di("wo", [D, D]); Bo = di("bo", [D])
        Wup = di("wup", [D_CROSS, D]); Bup = di("bup", [D_CROSS])
        Wdn = di("wdn", [D_PRIV, D_CROSS]); Bdn = di("bdn", [D_PRIV])
        Wqr = di("wqr", [1024, D_PRIV])
        KEYS = di("keys", [8, 128, 2, 64])
        DE = di("de", [N_EXPERTS, D_PRIV])
        UE = di("ue", [N_EXPERTS, D])
        EYEf = di("eyef", [128, 128])
        EYEb = di("eyeb", [128, 128], BF16)
        CAUS_KQ = di("caus_kq", [128, 128])
        CAUS_QK = di("caus_qk", [128, 128])
        ONESf = di("onesf", [1, 128])
        FOLDC = di("foldc", [8, 128, 32])
        NEGS = di("negs", [128, 1])
        ONESb = di("onesb", [1, 128], BF16)
        ONESc = di("onesc", [128, 1], BF16)
        OUT = nc.dram_tensor("out", [S, D], F32, kind="ExternalOutput")
        WupB = nc.dram_tensor("wupb", [D_CROSS, D], BF16)
        WdnB = nc.dram_tensor("wdnb", [D_PRIV, D_CROSS], BF16)
        WqrB = nc.dram_tensor("wqrb", [1024, D_PRIV], BF16)
        WoB = nc.dram_tensor("wob", [D, D], BF16)
        TBL = nc.dram_tensor("tbl", [N_EXPERTS, TROW], BF16)
        OALL = nc.dram_tensor("oall", [nT, NH, 128, 128], BF16)
        HTM = nc.dram_tensor("htm_s", [nT, 128, D_PRIV], BF16)
        ATTN = nc.dram_tensor("attn_s", [nT, 128, D], BF16)

        self.dbg = {}
        def dout(name, shape, dt=F32):
            if name in self.dbgnames:
                self.dbg[name] = nc.dram_tensor(
                    "dbg_" + name, shape, dt, kind="ExternalOutput")
            return self.dbg.get(name)
        D_XN = dout("xn", [nDC, 128, S], BF16)
        D_H = dout("h", [nPC, 128, S], BF16)
        D_HTM = dout("htm", [nT, 128, D_PRIV], BF16)
        D_Q = dout("q", [64, 16, 128], BF16)
        D_SIM = dout("sim", [nT, 128, 16, 128])
        D_S1V = dout("s1v", [nT, 128, 2, 8, 8])
        D_S1F = dout("s1f", [nT, 128, 2, 8, 8])
        D_ESEL = dout("esel", [128, nT, 64])
        D_GATES = dout("gates", [128, nT, 64])
        D_Z = dout("z", [nT, 128, 64])
        D_FFN = dout("ffn", [nT, 128, D])
        D_QH = dout("qh", [NH, 128, S], BF16)
        D_KH = dout("kh", [NH, 128, S], BF16)
        D_PT = dout("pt", [128, S], BF16)
        D_SUMS = dout("sums", [NH, 128, nT])
        D_ATT = dout("att", [nT, 128, D], BF16)

        def asb(name, shape, dt):
            return nc.alloc_sbuf_tensor(name, shape, dt)

        def aat(name, shape, dt, off):
            return nc.alloc_sbuf_tensor_at(name, shape, dt, offset=off)

        eyef = asb("eyef_s", [128, 128], F32)
        eyeb = asb("eyeb_s", [128, 128], BF16)
        caus_kq = asb("caus_kq_s", [128, 128], F32)
        caus_qk = asb("caus_qk_s", [128, 128], F32)
        onesf = asb("onesf_s", [1, 128], F32)
        foldc = asb("foldc_s", [128, 8, 32], F32)
        negs = asb("negs_s", [128, 1], F32)
        onesb = asb("onesb_s", [1, 128], BF16)
        onesc = asb("onesc_s", [128, 1], BF16)
        bias_up = asb("bias_up", [128, nCC], F32)
        bias_dn = asb("bias_dn", [128, nPC], F32)
        bias_q = asb("bias_q", [128, nDC], F32)
        bias_k = asb("bias_k", [128, nDC], F32)
        bv_sb = asb("bv_sb", [128, nDC], F32)
        bv_bb = asb("bv_bb", [128, nDC], BF16)
        stat = asb("stat", [128, 8], F32)
        stat2 = asb("stat2", [128, 16], F32)
        bias_stage = asb("bias_stage", [64, 128], F32)
        bstg_dn = asb("bstg_dn", [8, 128], F32)
        bstg_q = asb("bstg_q", [16, 128], F32)
        bstg_k = asb("bstg_k", [16, 128], F32)
        bstg_v = asb("bstg_v", [16, 128], F32)
        trash = asb("trash", [128, 1024], BF16)
        trash_b = asb("trash_b", [128, 1024], BF16)
        keysT = asb("keysT", [64, 16, 128], BF16)
        s1v = asb("s1v", [128, 2, 8, 8], F32)
        s1i = asb("s1i", [128, 2, 8, 8], U32)
        s1f = asb("s1f", [128, 2, 8, 8], F32)
        alls = asb("alls", [128, 8, 64], F32)
        scv = asb("scv", [128, 8, 8], F32)
        gexp = asb("gexp", [128, 8, 8], F32)
        sum8 = asb("sum8", [128, 8], F32)
        pk32 = asb("pk32", [128, 8, 8], U32)
        ku32 = asb("ku32", [128, 8, 8], U32)
        kxf = asb("kxf", [128, 8, 8], F32)
        kyf = asb("kyf", [128, 8, 8], F32)
        sel_t = asb("sel_t", [128, 8, 8], F32)
        sel_x = asb("sel_x", [128, 8, 8], F32)
        sel_y = asb("sel_y", [128, 8, 8], F32)
        msk8 = asb("msk8", [128, 8, 8], F32)
        esel = asb("esel", [128, nT, 64], F32)
        gates = asb("gates", [128, nT, 64], F32)
        sums_h = asb("sums_h", [128, nT], F32)
        sumsp = asb("sumsp", [128, 8], F32)
        recip_h = asb("recip_h", [128, nT], F32)
        idx_t = [asb(f"idx_t{i}", [128, 512], I16) for i in range(2)]
        zcol = asb("zcol", [128, 64], F32)
        zw = asb("zw", [128, 64], F32)
        gcol = asb("gcol", [128, 64], F32)

        ln_mean = asb("ln_mean", [128, 1], F32)
        ln_var = asb("ln_var", [128, 1], F32)
        ln_rstd = asb("ln_rstd", [128, 1], F32)
        atl = [asb(f"atl{i}", [128, 512], BF16) for i in range(2)]

        TOPB = 229120
        R_A = TOPB - 65536
        R_B = R_A - 65536
        R_C = R_B - 45056

        def overlay(name, shape, dt, off):
            return aat(name, shape, dt, off)

        xn = overlay("xn", [128, nDC, S], BF16, R_A)
        gbuf = [overlay(f"gbuf{i}", [128, SLOTC, TROW], BF16,
                        R_A + i * (SLOTC * TROW * 2)) for i in range(NGB)]
        h_f32 = overlay("h_f32", [128, nPC, S], F32, R_B)
        h_fm = overlay("h_fm", [128, nPC, S], BF16, R_B)
        h_tm = overlay("h_tm", [128, nT, D_PRIV], BF16, R_B + 32768)
        qh = overlay("qh", [128, S], BF16, R_B)
        kh = overlay("kh", [128, S], BF16, R_B + 4096)
        vfm = overlay("vfm", [128, S], F32, R_B + 8192)
        vtm = overlay("vtm", [128, nDC, 128], BF16, R_B + 16384)
        ptil = overlay("ptil", [128, S], BF16, R_B + 20480)
        wsl_pp = [overlay(f"wsl_pp{i}", [128, nDC, 128], BF16,
                          R_B + 24576 + i * 4096) for i in range(2)]
        sdiag = [overlay(f"sdiag{i}", [128, 128], BF16,
                         R_B + 16384 + i * 256) for i in range(8)]
        bo_row = overlay("bo_row", [1, D], F32, R_B)
        bo_rowb = overlay("bo_rowb", [1, D], BF16, R_B + 8192)
        bop_row = overlay("bop_row", [1, D], BF16, R_C + 40960)
        ztrash = [overlay(f"ztrash{i}", [128, 1024], BF16,
                          R_B + 24576 + i * 2048) for i in range(4)]
        wo_tile = overlay("wo_tile", [128, NH, 512], BF16, R_B)
        otl1 = overlay("otl1", [128, NH, 128], BF16, R_B + 18432)
        x_t = [overlay(f"x_t{i}", [128, D], F32, R_C + i * 8192)
               for i in range(2)]
        xn_tm = [overlay("xn_tm0", [128, D], F32, R_C + 16384),
                 overlay("xn_tm1", [128, D], F32, R_B)]
        wup_g = [overlay(f"wup_g{i}", [128, nDC, CG * 128], BF16,
                         R_C + i * 16384) for i in range(2)]
        wdn_g = overlay("wdn_g", [128, CG, D_PRIV], BF16, R_C + 32768)
        h1g = overlay("h1g", [128, CG, 512], BF16, R_C + 40960)
        wqr_t = overlay("wqr_t", [128, 16 * nPC, 64], BF16, R_C)
        q64 = overlay("q64", [64, 16, 128], BF16, R_C + 16384)
        sim_vb = overlay("sim_vb", [128, 16, 128], F32, R_C + 20480)
        keys_raw = overlay("keys_raw", [128, 16, 64], F32, R_C + 28672)
        htm_stage = [overlay(f"htm_stage{i}", [128, 1024], BF16,
                             R_C + 28672 + i * 2048) for i in range(2)]
        rrow = overlay("rrow", [1, 512], F32, R_C + 32768)
        rb_sb = overlay("rb_sb", [128, 512], F32, R_C + 34816)
        otile = [overlay(f"otile{i}", [128, 512], BF16,
                         R_C + 36864 + i * 1024) for i in range(2)]
        wo_cols = overlay("wo_cols", [128, NH, 512], BF16, R_C)
        otl = [overlay(f"otl{i}", [128, NH, 128], BF16, R_C + 16384 + i * 4096)
               for i in range(2)]
        att_t = [overlay(f"att_t{i}", [128, D], BF16, R_C + i * 4096)
                 for i in range(2)]
        xin_t = [overlay(f"xin_t{i}", [128, D], F32, R_C + 8192 + i * 8192)
                 for i in range(2)]
        out_t = [overlay(f"out_t{i}", [128, D], F32, R_C + 24576 + i * 8192)
                 for i in range(2)]

        guard = asb("alloc_guard", [128, 4], F32)
        gend = nc.lookup_mloc(guard).addr + 16
        assert gend <= R_C, f"smalls overflow into R_C: {gend} > {R_C}"

        self.psum = nc.alloc_psum_tensor("psum", [128, 4096], F32)
        self.bank_free = {i: (None, 0) for i in range(8)}

        self.s_x = s_x = self.mksem("s_x")
        self.s_mm = s_mm = self.mksem("s_mm")
        self.s_dve = s_dve = self.mksem("s_dve")
        self.s_act = s_act = self.mksem("s_act")
        self.s_out = s_out = self.mksem("s_out")
        s_outb = self.mksem("s_outb")
        s_xin = [self.mksem("s_xin0"), self.mksem("s_xin1")]
        s_cwup = self.mksem("s_cwup")
        s_cwdn = self.mksem("s_cwdn")
        s_cwqr = self.mksem("s_cwqr")
        s_cwo = self.mksem("s_cwo")
        s_ctbl = self.mksem("s_ctbl")
        s_wup = self.mksem("s_wup")
        s_wdn = self.mksem("s_wdn")
        s_wqr = self.mksem("s_wqr")
        s_wv2 = [self.mksem("s_wv0"), self.mksem("s_wv1")]
        s_wo = self.mksem("s_wo")
        s_otl = [self.mksem("s_otl0"), self.mksem("s_otl1")]
        s_ost = [self.mksem("s_ost0"), self.mksem("s_ost1")]
        s_ast = [self.mksem("s_ast0"), self.mksem("s_ast1")]
        s_gat2 = [self.mksem(f"s_gat{i}") for i in range(NGB)]
        s_asm = [self.mksem("s_asm0"), self.mksem("s_asm1")]
        s_idxL = [self.mksem("s_idxL0"), self.mksem("s_idxL1")]
        s_dbg = self.mksem("s_dbg")
        outsems = [s_out, s_outb]

        self._gregs = {}
        def _init_gp(g):
            g.load_library(MLP_LIB)
            self._gregs["nidx"] = g.to_reg(GCALL)
        ins("gpsimd", _init_gp)
        ld(eyef[:], EYEf[:])
        ld(eyeb[:], EYEb[:])
        ld(caus_kq[:], CAUS_KQ[:])
        ld(caus_qk[:], CAUS_QK[:])
        ld(onesf[:], ONESf[:])
        ld(foldc[:], FOLDC[:].rearrange("u t p -> t u p"))
        ld(negs[:], NEGS[:])
        ld(onesb[:], ONESb[:])
        ld(onesc[:], ONESc[:])
        bstage_v = ld(bias_stage[0:64, :],
                      Bup[:].rearrange("(c p) -> c p", p=128))
        ld(bstg_dn[:], Bdn[:].rearrange("(c p) -> c p", p=128))
        ld(bstg_q[:], Bq[:].rearrange("(c p) -> c p", p=128))
        ld(bstg_k[:], Bk[:].rearrange("(c p) -> c p", p=128))
        ld(bstg_v[:], Bv[:].rearrange("(c p) -> c p", p=128))
        consts_v = 0
        for pp in range(2):
            for hh in range(8):
                consts_v = ld(keys_raw[:, pp * 8 + hh, :], KEYS[hh, :, pp, :])

        def cast_dma(dst, src, pieces, sm):
            n = src.shape[0]
            stp = n // pieces
            for i in range(pieces):
                ins("gpsimd", lambda g, d=dst[i * stp:(i + 1) * stp],
                    s=src[i * stp:(i + 1) * stp]: g.dma_start(out=d, in_=s),
                    sm, 16)
            return sm.n

        wup_cast_v = cast_dma(WupB, Wup, 8, s_cwup)
        wdn_cast_v = cast_dma(WdnB, Wdn, 4, s_cwdn)
        wqr_cast_v = cast_dma(WqrB, Wqr, 1, s_cwqr)
        wo_cast_v = cast_dma(WoB, Wo, 2, s_cwo) if st >= 4 else 0

        wait("tensor", s_x, consts_v)
        self.transpose_evac(bias_stage[0:64, :], bias_up[:], 6, eyef[:])
        self.transpose_evac(bstg_dn[:], bias_dn[:], 7, eyef[:])
        self.transpose_evac(bstg_q[:], bias_q[:], 6, eyef[:])
        self.transpose_evac(bstg_k[:], bias_k[:], 7, eyef[:])
        self.transpose_evac(bstg_v[:], bv_sb[:], 6, eyef[:])

        x_free_d = [0, 0]
        x_free_a = [0, 0]
        xn_tm_free = [0, 0]
        for T in range(nT):
            bi = T % 2
            b = 8 * bi
            wait("sync", s_dve, x_free_d[bi])
            wait("sync", s_act, x_free_a[bi])
            xv = ld(x_t[bi][:], X[128 * T:128 * T + 128, :], sem=s_xin[bi])
            xt = x_t[bi]
            wait("vector", s_xin[bi], xv)
            ins("vector", lambda v, xt=xt, b=b: v.reduce_sum(
                stat2[:, b + 0:b + 1], xt[:], axis=AX.X), s_dve)
            sum_v = s_dve.n
            wait("scalar", s_xin[bi], xv)
            ins("scalar", lambda sc, xt=xt, b=b: sc.activation(
                out=trash[:, 0:1024], in_=xt[:, 0:1024], func=AF.Square,
                accum_out=stat2[:, b + 1:b + 2]), s_act)
            ins("scalar", lambda sc, xt=xt, b=b: sc.activation(
                out=trash_b[:, 0:1024], in_=xt[:, 1024:2048],
                func=AF.Square,
                accum_out=stat2[:, b + 2:b + 3]), s_act)
            act_sq_v = s_act.n
            x_free_a[bi] = act_sq_v
            wait("vector", s_dve, sum_v)
            wait("vector", s_act, act_sq_v)
            ins("vector", lambda v, b=b: v.tensor_scalar(
                out=stat2[:, b + 3:b + 4], in0=stat2[:, b + 0:b + 1],
                scalar1=1.0 / D, scalar2=None, op0=ALU.mult))
            ins("vector", lambda v, b=b: v.tensor_tensor(
                out=stat2[:, b + 4:b + 5], in0=stat2[:, b + 1:b + 2],
                in1=stat2[:, b + 2:b + 3], op=ALU.add), s_dve)
            wait("vector", s_dve, s_dve.n)
            ins("vector", lambda v, b=b: v.tensor_tensor(
                out=stat2[:, b + 5:b + 6], in0=stat2[:, b + 3:b + 4],
                in1=stat2[:, b + 3:b + 4], op=ALU.mult), s_dve)
            wait("vector", s_dve, s_dve.n)
            ins("vector", lambda v, b=b: v.scalar_tensor_tensor(
                out=stat2[:, b + 6:b + 7], in0=stat2[:, b + 4:b + 5],
                scalar=1.0 / D, in1=stat2[:, b + 5:b + 6],
                op0=ALU.mult, op1=ALU.subtract), s_dve)
            wait("vector", s_dve, s_dve.n)
            ins("vector", lambda v, b=b: v.tensor_scalar(
                out=stat2[:, b + 6:b + 7], in0=stat2[:, b + 6:b + 7],
                scalar1=1e-5, scalar2=None, op0=ALU.add), s_dve)
            wait("vector", s_dve, s_dve.n)
            ins("vector", lambda v, b=b: v.reciprocal(
                out=stat2[:, b + 6:b + 7], in_=stat2[:, b + 6:b + 7]), s_dve)
            varpe_v = s_dve.n
            wait("scalar", s_dve, varpe_v)
            ins("scalar", lambda sc, b=b: sc.activation(
                out=stat2[:, b + 7:b + 8], in_=stat2[:, b + 6:b + 7],
                func=AF.Sqrt), s_act)
            rstd_v = s_act.n
            wait("vector", s_act, rstd_v)
            ins("vector", lambda v, b=b: v.tensor_tensor(
                out=stat2[:, b + 5:b + 6], in0=stat2[:, b + 3:b + 4],
                in1=stat2[:, b + 7:b + 8], op=ALU.mult), s_dve)
            wait("vector", s_dve, s_dve.n)
            wait("vector", s_mm, xn_tm_free[bi])
            ins("vector", lambda v, xt=xt, bi=bi, b=b: v.tensor_scalar(
                out=xn_tm[bi][:], in0=xt[:], scalar1=stat2[:, b + 7:b + 8],
                scalar2=stat2[:, b + 5:b + 6],
                op0=ALU.mult, op1=ALU.subtract), s_dve)
            big_v = s_dve.n
            x_free_d[bi] = big_v
            wait("tensor", s_dve, big_v)
            for j in range(nDC):
                self.transpose_evac(
                    xn_tm[bi][:, 128 * j:128 * j + 128],
                    xn[:, j, 128 * T:128 * T + 128], 6 + (j % 2), eyef[:],
                    eng="vector" if j % 2 == 0 else "scalar")
            xn_tm_free[bi] = s_mm.n
        xn_done_dve = s_dve.n
        xn_done_act = s_act.n

        if D_XN is not None:
            wait("scalar", s_dve, xn_done_dve)
            wait("scalar", s_act, xn_done_act)
            for j in range(nDC):
                self.st(D_XN[j], xn[:, j, :], sem=s_dbg)

        if st < 2:
            self.epilogue(OUT, out_t, write_dummy=True,
                          extra_sems=(s_dbg,))
            return

        for sl in range(16):
            self.transpose_evac(keys_raw[:, sl, :], keysT[:, sl, :], 6,
                                eyef[:])
        keysT_v = s_dve.n

        wup_busy = [0, 0]
        wdn_busy_v = 0
        hacc_evac_v = 0
        h1_evac = {6: self.bank_free[6], 7: self.bank_free[7]}
        wait("sync", s_dve, keysT_v)
        wait("sync", s_act, xn_done_act)
        wait("tensor", s_dve, xn_done_dve)
        wait("tensor", s_act, xn_done_act)
        s_wup2 = [self.mksem("s_wup0"), self.mksem("s_wup1")]
        wup_seen = [0, 0]
        for cg in range(nCC // CG):
            gbn = cg % 2
            wait("sync", s_mm, wup_busy[gbn])
            wait("sync", s_cwup,
                 16 * ((128 * CG * (cg + 1) - 1) // (D_CROSS // 8) + 1))
            if cg == 0:
                wait("sync", s_cwdn, wdn_cast_v)
            wup_v = wdn_v = 0
            for d in range(nDC):
                wup_v = ldT(wup_g[gbn][:, d, :],
                            WupB[128 * CG * cg:128 * CG * (cg + 1),
                                 128 * d:128 * d + 128], sem=s_wup2[gbn])
            wait("sync", s_mm, wup_busy[1 - gbn])
            for ci in range(CG):
                c = cg * CG + ci
                wdn_v = ldT(wdn_g[:, ci, :],
                            WdnB[0:D_PRIV, 128 * c:128 * c + 128], sem=s_wdn)
            for sp in range(4):
                tok0 = 512 * sp
                wait("tensor", s_wup2[gbn], wup_v)
                wait("tensor", s_dve, hacc_evac_v)
                for ci in range(CG):
                    bk_i = 6 + (ci % 2)
                    self.waitt("tensor", h1_evac[bk_i])
                    for d in range(nDC):
                        ins("tensor", lambda t, ci=ci, d=d, b=bk_i, tok0=tok0,
                            gbn=gbn: t.matmul(
                                self.bank(b),
                                wup_g[gbn][:, d, 128 * ci:128 * ci + 128],
                                xn[:, d, tok0:tok0 + 512],
                                start=(d == 0), stop=(d == nDC - 1)),
                            s_mm if d == nDC - 1 else None)
                    mmv = s_mm.n
                    wait("scalar", s_mm, mmv)
                    wait("scalar", s_mm, wdn_busy_v)
                    ins("scalar", lambda sc, ci=ci, b=bk_i, c=cg * CG + ci:
                        sc.activation(out=h1g[:, ci, :], in_=self.bank(b),
                                      func=AF.Silu,
                                      bias=bias_up[:, c:c + 1]), s_act)
                    h1_evac[bk_i] = (s_act, s_act.n)
                h1gv = s_act.n
                wait("tensor", s_act, h1gv)
                wait("tensor", s_wdn, wdn_v)
                wait("tensor", s_dve, hacc_evac_v)
                for pc in range(nPC):
                    for ci in range(CG):
                        ins("tensor", lambda t, pc=pc, ci=ci: t.matmul(
                            self.bank(pc),
                            wdn_g[:, ci, 128 * pc:128 * pc + 128],
                            h1g[:, ci, :],
                            start=(ci == 0), stop=(ci == CG - 1)),
                            s_mm if ci == CG - 1 else None)
                wdn_busy_v = s_mm.n
                mmv = s_mm.n
                wait("vector", s_mm, mmv)
                for pc in range(nPC):
                    if cg == 0:
                        ins("vector", lambda v, pc=pc, tok0=tok0:
                            v.tensor_copy(
                                out=h_f32[:, pc, tok0:tok0 + 512],
                                in_=self.bank(pc)), s_dve)
                    else:
                        ins("vector", lambda v, pc=pc, tok0=tok0:
                            v.tensor_tensor(
                                out=h_f32[:, pc, tok0:tok0 + 512],
                                in0=h_f32[:, pc, tok0:tok0 + 512],
                                in1=self.bank(pc), op=ALU.add), s_dve)
                hacc_evac_v = s_dve.n
            wup_busy[gbn] = s_mm.n
        if st >= 3:
            wait("gpsimd", s_wup2[0], s_wup2[0].n)
            wait("gpsimd", s_wup2[1], s_wup2[1].n)
            wait("gpsimd", s_wdn, s_wdn.n)
            cast_dma(TBL[:, 0:D_PRIV], DE, 8, s_ctbl)
            cast_dma(TBL[:, D_PRIV:TROW], UE, 16, s_ctbl)
        tbl_cast_v = s_ctbl.n
        for i in range(8):
            self.bank_free[i] = (s_dve, hacc_evac_v)
        self.bank_free[6] = h1_evac[6]
        self.bank_free[7] = h1_evac[7]
        wait("vector", s_dve, s_dve.n)
        for pc in range(nPC):
            ins("vector", lambda v, pc=pc: v.tensor_scalar(
                out=h_f32[:, pc, :], in0=h_f32[:, pc, :],
                scalar1=bias_dn[:, pc:pc + 1], scalar2=None, op0=ALU.add),
                s_dve)
        hbias_v = s_dve.n
        wait("tensor", s_dve, hbias_v)
        hst_free = [(None, 0), (None, 0)]
        s_htm = self.mksem("s_htm")
        s_htr = [self.mksem("s_htr0"), self.mksem("s_htr1")]
        for T in range(nT):
            r = T % 2
            self.waitt("vector", hst_free[r])
            for pc in range(nPC):
                self.transpose_evac(h_f32[:, pc, 128 * T:128 * T + 128],
                                    htm_stage[r][:, 128 * pc:128 * pc + 128],
                                    6 + (pc % 2), eyef[:])
            hv = s_dve.n
            wait("scalar", s_dve, hv)
            sv = self.st(HTM[T], htm_stage[r][:], sem=s_htr[r])
            hst_free[r] = (s_htr[r], sv)
        htm_stored_v = [s_htr[0].n, s_htr[1].n]
        htr_mm_v = s_mm.n
        wait("vector", s_mm, htr_mm_v)
        ins("vector", lambda v: v.tensor_copy(
            out=trash[:, 0:1024], in_=h_f32[:, 0, 0:1024]))
        ins("vector", lambda v: v.tensor_copy(
            out=trash_b[:, 0:1024], in_=h_f32[:, 0, 1024:2048]), s_dve)
        wait("vector", s_dve, s_dve.n)
        ins("vector", lambda v: v.tensor_copy(
            out=h_fm[:, 0, 0:1024], in_=trash[:, 0:1024]))
        ins("vector", lambda v: v.tensor_copy(
            out=h_fm[:, 0, 1024:2048], in_=trash_b[:, 0:1024]), s_dve)
        wait("vector", s_dve, s_dve.n)
        for pc in range(1, nPC):
            ins("vector", lambda v, pc=pc: v.tensor_copy(
                out=h_fm[:, pc, :], in_=h_f32[:, pc, :]), s_dve)
            wait("vector", s_dve, s_dve.n)
        h_fm_v = s_dve.n
        wait("sync", s_htr[0], htm_stored_v[0])
        wait("sync", s_htr[1], htm_stored_v[1])
        wait("sync", s_dve, h_fm_v)
        htm_ld_v = ld(h_tm[:], HTM[:].rearrange("t p x -> p t x"),
                      sem=s_htm)
        h_tm_v = None
        if D_H is not None:
            wait("scalar", s_dve, h_fm_v)
            for pc in range(nPC):
                self.st(D_H[pc], h_fm[:, pc, :], sem=s_dbg)
        if D_HTM is not None:
            wait("scalar", s_htm, htm_ld_v)
            for T in range(nT):
                self.st(D_HTM[T], h_tm[:, T, :], sem=s_dbg)

        wait("sync", s_cwqr, wqr_cast_v)
        wait("sync", s_mm, max(wup_busy))
        wqr_v = 0
        for sl in range(16):
            for pc in range(nPC):
                wqr_v = ldT(wqr_t[:, sl * nPC + pc, :],
                            WqrB[64 * sl:64 * sl + 64,
                                 128 * pc:128 * pc + 128], sem=s_wqr)

        def insv(fn):
            ins("vector", fn, s_dve)
            wait("vector", s_dve, s_dve.n)

        routing_done_v = 0
        dbgq_v = 0
        for T in range(nT):
            tok0 = 128 * T
            wait("tensor", s_wqr, wqr_v)
            if dbgq_v:
                wait("vector", s_dbg, s_dbg.n)
            for sl in range(16):
                bk_i = 6 + (sl % 2)
                self.bwait("tensor", bk_i)
                for pc in range(nPC):
                    ins("tensor", lambda t, sl=sl, pc=pc, b=bk_i, tok0=tok0:
                        t.matmul(self.bank(b, 0, 128)[0:64, :],
                                 wqr_t[:, sl * nPC + pc, :],
                                 h_fm[:, pc, tok0:tok0 + 128],
                                 start=(pc == 0), stop=(pc == nPC - 1)),
                        s_mm if pc == nPC - 1 else None)
                mmv = s_mm.n
                wait("vector", s_mm, mmv)
                ins("vector", lambda v, sl=sl, b=bk_i: v.tensor_copy(
                    out=q64[:, sl, :], in_=self.bank(b, 0, 128)[0:64, :]),
                    s_dve)
                self.bset(bk_i, s_dve)
            q64_v = s_dve.n
            if D_Q is not None and T == 0:
                wait("scalar", s_dve, q64_v)
                self.st(D_Q[:], q64[:], sem=s_dbg)
                dbgq_v = s_dbg.n
            wait("tensor", s_dve, max(q64_v, keysT_v))
            for sl in range(16):
                bk_i = 6 + (sl % 2)
                self.bwait("tensor", bk_i)
                ins("tensor", lambda t, sl=sl, b=bk_i: t.matmul(
                    self.bank(b, 0, 128), q64[:, sl, :], keysT[:, sl, :],
                    start=True, stop=True), s_mm)
                mmv = s_mm.n
                wait("vector", s_mm, mmv)
                ins("vector", lambda v, sl=sl, b=bk_i: v.tensor_copy(
                    out=sim_vb[:, sl, :], in_=self.bank(b, 0, 128)), s_dve)
                self.bset(bk_i, s_dve)
            sim_v = s_dve.n
            if D_SIM is not None:
                wait("scalar", s_dve, sim_v)
                self.st(D_SIM[T], sim_vb[:], sem=s_dbg)
            wait("vector", s_dve, s_dve.n)
            for pp in range(2):
                for h in range(8):
                    ins("vector", lambda v, pp=pp, h=h: v.max(
                        out=s1v[:, pp, h, :], in_=sim_vb[:, pp * 8 + h, :]),
                        s_dve)
            wait("vector", s_dve, s_dve.n)
            for pp in range(2):
                for h in range(8):
                    ins("vector", lambda v, pp=pp, h=h: v.max_index(
                        out=s1i[:, pp, h, :], in_max=s1v[:, pp, h, :],
                        in_values=sim_vb[:, pp * 8 + h, :]), s_dve)
            wait("vector", s_dve, s_dve.n)
            insv(lambda v: v.tensor_copy(out=s1f[:], in_=s1i[:]))
            if D_S1V is not None:
                ins("vector", lambda v: v.engine_nop(), s_dve)
                wait("scalar", s_dve, s_dve.n)
                self.st(D_S1V[T], s1v[:], sem=s_dbg)
                self.st(D_S1F[T], s1f[:], sem=s_dbg)
            for h in range(8):
                ins("vector", lambda v, h=h: v.tensor_tensor(
                    out=alls[:, h, :].rearrange("p (a b) -> p a b", a=8),
                    in0=s1v[:, 0, h, :].rearrange("p (a o) -> p a o", o=1
                        ).to_broadcast([128, 8, 8]),
                    in1=s1v[:, 1, h, :].rearrange("p (o b) -> p o b", o=1
                        ).to_broadcast([128, 8, 8]),
                    op=ALU.add), s_dve)
            wait("vector", s_dve, s_dve.n)
            for h in range(8):
                ins("vector", lambda v, h=h: v.max(
                    out=scv[:, h, :], in_=alls[:, h, :]), s_dve)
            wait("vector", s_dve, s_dve.n)
            for h in range(8):
                ins("vector", lambda v, h=h: v.max_index(
                    out=pk32[:, h, :], in_max=scv[:, h, :],
                    in_values=alls[:, h, :]), s_dve)
            wait("vector", s_dve, s_dve.n)
            insv(lambda v: v.tensor_single_scalar(
                out=ku32[:], in_=pk32[:], scalar=3,
                op=ALU.logical_shift_right))
            insv(lambda v: v.tensor_copy(out=kxf[:], in_=ku32[:]))
            insv(lambda v: v.tensor_single_scalar(
                out=ku32[:], in_=pk32[:], scalar=7, op=ALU.bitwise_and))
            insv(lambda v: v.tensor_copy(out=kyf[:], in_=ku32[:]))
            for a in range(8):
                insv(lambda v, a=a: v.tensor_single_scalar(
                    out=msk8[:], in_=kxf[:], scalar=float(a),
                    op=ALU.is_equal))
                insv(lambda v, a=a: v.tensor_tensor(
                    out=sel_t[:], in0=msk8[:],
                    in1=s1f[:, 0, :, a].rearrange("p (h o) -> p h o", o=1
                        ).to_broadcast([128, 8, 8]),
                    op=ALU.mult))
                if a == 0:
                    insv(lambda v: v.tensor_copy(
                        out=sel_x[:], in_=sel_t[:]))
                else:
                    insv(lambda v: v.tensor_tensor(
                        out=sel_x[:], in0=sel_x[:], in1=sel_t[:], op=ALU.add))
                insv(lambda v, a=a: v.tensor_single_scalar(
                    out=msk8[:], in_=kyf[:], scalar=float(a),
                    op=ALU.is_equal))
                insv(lambda v, a=a: v.tensor_tensor(
                    out=sel_t[:], in0=msk8[:],
                    in1=s1f[:, 1, :, a].rearrange("p (h o) -> p h o", o=1
                        ).to_broadcast([128, 8, 8]),
                    op=ALU.mult))
                if a == 0:
                    insv(lambda v: v.tensor_copy(
                        out=sel_y[:], in_=sel_t[:]))
                else:
                    insv(lambda v: v.tensor_tensor(
                        out=sel_y[:], in0=sel_y[:], in1=sel_t[:], op=ALU.add))
            ins("vector", lambda v, T=T: v.scalar_tensor_tensor(
                out=esel[:, T, :].rearrange("p (h j) -> p h j", h=8),
                in0=sel_x[:], scalar=128.0, in1=sel_y[:],
                op0=ALU.mult, op1=ALU.add), s_dve)
            esel_v = s_dve.n
            wait("scalar", s_dve, esel_v)
            ins("scalar", lambda sc: sc.activation(
                out=gexp[:], in_=scv[:], func=AF.Exp), s_act)
            wait("vector", s_act, s_act.n)
            for h in range(8):
                ins("vector", lambda v, h=h: v.reduce_sum(
                    sum8[:, h:h + 1], gexp[:, h, :], axis=AX.X),
                    s_dve if h == 7 else None)
            wait("vector", s_dve, s_dve.n)
            ins("vector", lambda v: v.reciprocal(out=sum8[:], in_=sum8[:]),
                s_dve)
            wait("vector", s_dve, s_dve.n)
            ins("vector", lambda v, T=T: v.tensor_tensor(
                out=gates[:, T, :].rearrange("p (h j) -> p h j", h=8),
                in0=gexp[:],
                in1=sum8[:].rearrange("p (h o) -> p h o", o=1
                    ).to_broadcast([128, 8, 8]),
                op=ALU.mult), s_dve)
            routing_done_v = s_dve.n

        if D_ESEL is not None:
            wait("scalar", s_dve, routing_done_v)
            self.st(D_ESEL[:], esel[:], sem=s_dbg)
        if D_GATES is not None:
            wait("scalar", s_dve, routing_done_v)
            self.st(D_GATES[:], gates[:], sem=s_dbg)

        h_fm_last_use = s_mm.n
        xn_last_use = s_mm.n
        p35_psum_free = (None, 0)
        attn_done_v = 0

        if st >= 4:
            wsl_busy = [(None, 0), (None, 0)]
            wsl_load_v = [0, 0]
            qkv_banks_free = (None, 0)
            wait("gpsimd", s_mm, h_fm_last_use)
            projs = [(h, pj, W) for h in range(NH)
                     for pj, W in (("q", Wq), ("k", Wk), ("v", Wv))]

            def proj_load(i):
                h2, pj2, Wmat2 = projs[i]
                pb = i % 2
                self.waitt("gpsimd", wsl_busy[pb])
                wait("gpsimd", s_wv2[pb], wsl_load_v[pb])
                for d in range(nDC):
                    self.ins("gpsimd", lambda g, d=d, pb=pb, h2=h2,
                             Wmat2=Wmat2: g.dma_start(
                                 out=wsl_pp[pb][:, d, :],
                                 in_=Wmat2[128 * d:128 * d + 128,
                                           128 * h2:128 * h2 + 128]),
                             s_wv2[pb], 16)
                wsl_load_v[pb] = s_wv2[pb].n

            proj_load(0)
            for h in range(NH):
                for pjn, (proj, Wmat) in enumerate(
                        (("q", Wq), ("k", Wk), ("v", Wv))):
                    i = 3 * h + pjn
                    pb = i % 2
                    if i + 1 < len(projs):
                        proj_load(i + 1)
                    self.waitt("tensor", qkv_banks_free)
                    wait("tensor", s_wv2[pb], wsl_load_v[pb])
                    for d in range(nDC):
                        for sg in range(4):
                            ins("tensor", lambda t, d=d, sg=sg, pb=pb:
                                t.matmul(
                                    self.bank(sg), wsl_pp[pb][:, d, :],
                                    xn[:, d, 512 * sg:512 * sg + 512],
                                    start=(d == 0), stop=(d == nDC - 1)),
                                s_mm if (d == nDC - 1 and sg == 3) else None)
                    wsl_busy[pb] = (s_mm, s_mm.n)
                    xn_last_use = s_mm.n
                    mmv = s_mm.n
                    wait("scalar", s_mm, mmv)
                    if proj == "q":
                        ins("scalar", lambda sc, h=h: sc.activation(
                            out=qh[:], in_=self.psum[:, 0:2048],
                            func=AF.Identity, bias=bias_q[:, h:h + 1]),
                            s_act)
                    elif proj == "k":
                        ins("scalar", lambda sc, h=h: sc.activation(
                            out=kh[:], in_=self.psum[:, 0:2048],
                            func=AF.Identity, bias=bias_k[:, h:h + 1]),
                            s_act)
                    else:
                        ins("scalar", lambda sc: sc.copy(
                            out=vfm[:], in_=self.psum[:, 0:2048]), s_act)
                    qkv_banks_free = (s_act, s_act.n)
                qkv_v = s_act.n
                if D_QH is not None:
                    wait("scalar", s_act, qkv_v)
                    self.st(D_QH[h], qh[:], sem=s_dbg)
                    self.st(D_KH[h], kh[:], sem=s_dbg)
                wait("tensor", s_act, qkv_v)
                for kc in range(nDC):
                    self.transpose_evac(vfm[:, 128 * kc:128 * kc + 128],
                                        vtm[:, kc, :], 6 + (kc % 2), eyef[:])
                vtm_v = s_dve.n
                self.waitt("tensor", qkv_banks_free)
                for sg2 in range(4):
                    sb_i = 6 + (sg2 % 2)
                    self.bwait("tensor", sb_i)
                    self.bwait("tensor", sg2)
                    nkc = 4 * sg2 + 4

                    def a2(kc, sg2=sg2):
                        off = max(0, 128 * kc - 512 * sg2)
                        w = 512 - off
                        bk_i = 4 + (kc % 2)
                        self.bwait("tensor", bk_i)
                        ins("tensor", lambda t, kc=kc, sg2=sg2, off=off, w=w,
                            b=bk_i: t.matmul(
                                self.bank(b, off, w),
                                kh[:, 128 * kc:128 * kc + 128],
                                qh[:, 512 * sg2 + off:512 * sg2 + off + w],
                                start=True, stop=True), s_mm)
                        mmv = s_mm.n
                        if kc >= 4 * sg2:
                            wait("vector", s_mm, mmv)
                            ins("vector", lambda v, b=bk_i, off=off:
                                v.tensor_tensor(
                                    out=self.bank(b, off, 128),
                                    in0=self.bank(b, off, 128),
                                    in1=caus_kq[:], op=ALU.add), s_dve)
                            wait("scalar", s_dve, s_dve.n)
                        else:
                            wait("scalar", s_mm, mmv)
                        ins("scalar", lambda sc, b=bk_i, off=off, w=w,
                            kc=kc: sc.activation(
                                out=ptil[:, 512 * (kc % 2) + off:
                                         512 * (kc % 2) + off + w],
                                in_=self.bank(b, off, w), func=AF.Exp,
                                scale=INV_SQRT_HD, bias=negs[:]), s_act)
                        self.bset(bk_i, s_act)
                        return s_act.n

                    ptv = a2(0)
                    for kc in range(nkc):
                        off = max(0, 128 * kc - 512 * sg2)
                        w = 512 - off
                        nxt = a2(kc + 1) if kc + 1 < nkc else None
                        wait("tensor", s_act, ptv)
                        if kc == 0 and sg2 == 0:
                            wait("tensor", s_dve, vtm_v)
                        last = (kc == nkc - 1)
                        ins("tensor", lambda t, kc=kc, sg2=sg2, off=off, w=w,
                            last=last: t.matmul(
                                self.bank(sg2, off, w), vtm[:, kc, :],
                                ptil[:, 512 * (kc % 2) + off:
                                     512 * (kc % 2) + off + w],
                                start=(kc == 0), stop=last),
                            s_mm if last else None)
                        ins("tensor", lambda t, kc=kc, off=off, w=w,
                            b=sb_i, last=last: t.matmul(
                                self.bank(b, off, w)[0:1, :], onesc[:],
                                ptil[:, 512 * (kc % 2) + off:
                                     512 * (kc % 2) + off + w],
                                start=(kc == 0), stop=last),
                            s_mm if last else None)
                        ptv = nxt
                    av_done = s_mm.n
                    wait("vector", s_mm, av_done)
                    ins("vector", lambda v, b=sb_i: v.tensor_copy(
                        out=rrow[:], in_=self.bank(b, 0, 512)[0:1, :]),
                        s_dve)
                    wait("vector", s_dve, s_dve.n)
                    ins("vector", lambda v: v.reciprocal(
                        out=rrow[:], in_=rrow[:]), s_dve)
                    rrv = s_dve.n
                    self.bset(sb_i, s_dve)
                    self.bwait("tensor", sb_i)
                    wait("tensor", s_dve, rrv)
                    ins("tensor", lambda t, b=sb_i: t.matmul(
                        self.bank(b), onesf[:], rrow[:],
                        start=True, stop=True), s_mm)
                    mmv = s_mm.n
                    wait("vector", s_mm, mmv)
                    ins("vector", lambda v, b=sb_i: v.tensor_copy(
                        out=rb_sb[:], in_=self.bank(b)), s_dve)
                    self.bset(sb_i, s_dve)
                    wait("vector", s_dve, s_dve.n)
                    ot = otile[sg2 % 2]
                    if hasattr(self, "ot_store"):
                        wait("vector", s_ost[sg2 % 2],
                             self.ot_store[sg2 % 2])
                    ins("vector", lambda v, sg2=sg2, ot=ot:
                        v.scalar_tensor_tensor(
                            out=ot[:],
                            in0=self.bank(sg2, 0, 512),
                            scalar=1.0, in1=rb_sb[:],
                            op0=ALU.bypass, op1=ALU.mult), s_dve)
                    self.bset(sg2, s_dve)
                    ov = s_dve.n
                    wait("sync", s_dve, ov)
                    sv = self.st(
                        OALL[4 * sg2:4 * sg2 + 4, h].rearrange(
                            "t p x -> p t x"),
                        ot[:].rearrange("p (t x) -> p t x", t=4),
                        sem=s_ost[sg2 % 2], eng="sync")
                    if not hasattr(self, "ot_store"):
                        self.ot_store = [0, 0]
                    self.ot_store[sg2 % 2] = sv
                qkv_banks_free = (s_dve, s_dve.n)

            wait("sync", s_cwo, wo_cast_v)
            wait("sync", s_act, s_act.n)
            wait("sync", s_mm, s_mm.n)
            s_bol = self.mksem("s_bol")
            bol_v = ld(bo_row[:], Bo[:].rearrange("(o d) -> o d", o=1),
                       sem=s_bol)
            wait("vector", s_bol, bol_v)
            ins("vector", lambda v: v.tensor_copy(
                out=bo_rowb[:], in_=bo_row[:]), s_dve)
            ins("vector", lambda v: v.tensor_copy(
                out=bv_bb[:], in_=bv_sb[:]), s_dve)
            borow_v = s_dve.n
            wo_v = 0
            wait("sync", s_mm, s_mm.n)
            wo_first_v = None
            p35_bank_free = [(s_act, s_act.n), (s_act, s_act.n)]
            att_sv = [0, 0]
            oall_done_v = [s_ost[0].n, s_ost[1].n]
            for dsg in range(4):
                wait("sync", s_mm, s_mm.n if dsg > 0 else 0)
                for hc in range(NH):
                    wo_v = ld(wo_cols[:, hc, :],
                              WoB[128 * hc:128 * hc + 128,
                                  512 * dsg:512 * dsg + 512], sem=s_wo)
                self.bwait("tensor", 6)
                wait("tensor", s_wo, wo_v)
                if dsg == 0:
                    wait("tensor", s_dve, borow_v)
                for hc in range(NH):
                    ins("tensor", lambda t, hc=hc: t.matmul(
                        self.bank(6)[0:1, :], bv_bb[:, hc:hc + 1],
                        wo_cols[:, hc, :], start=(hc == 0), stop=False), None)
                ins("tensor", lambda t, dsg=dsg: t.matmul(
                    self.bank(6)[0:1, :], onesb[0:1, 0:1],
                    bo_rowb[:, 512 * dsg:512 * dsg + 512],
                    start=False, stop=True), s_mm)
                mmv = s_mm.n
                wait("vector", s_mm, mmv)
                ins("vector", lambda v, dsg=dsg: v.tensor_copy(
                    out=bop_row[:, 512 * dsg:512 * dsg + 512],
                    in_=self.bank(6)[0:1, :]), s_dve)
                self.bset(6, s_dve)
                bop_v = s_dve.n
            wait("sync", s_ost[0], oall_done_v[0])
            wait("sync", s_ost[1], oall_done_v[1])
            p35_psum_free = (s_dve, s_dve.n)

        if st < 3:
            self.epilogue(OUT, out_t, write_dummy=True,
                          extra_sems=(s_dbg,))
            return

        self._dbgh = dict(TBL=TBL, idx_t=idx_t, gbuf=gbuf)
        wait("gpsimd", s_mm, xn_last_use)
        wait("gpsimd", s_ctbl, tbl_cast_v)
        wait("gpsimd", s_cwup, s_cwup.n)
        wait("gpsimd", s_cwdn, s_cwdn.n)
        wait("gpsimd", s_cwqr, s_cwqr.n)
        if st >= 4:
            wait("gpsimd", s_cwo, s_cwo.n)
        wait("vector", s_htm, htm_ld_v)
        ins("vector", lambda v: v.memset(idx_t[0][:], 0))
        ins("vector", lambda v: v.memset(idx_t[1][:], 0), s_dve)
        wait("vector", s_dve, s_dve.n)
        wait("sync", s_mm, s_mm.n)
        wait("sync", s_dve, s_dve.n)
        wait("sync", s_act, s_act.n)
        wait("sync", s_dbg, s_dbg.n)
        self.waitt("tensor", p35_psum_free)
        idx_free = [(None, 0), (None, 0)]
        gbuf_free = [(None, 0) for _ in range(NGB)]
        ffn_banks_v = 0
        asm_free = [(None, 0), (None, 0)]
        last_gather_v = [0] * NGB
        self.sdguard = [0, 0, 0, 0]
        gcall = 0
        self.wo_busy_v = 0
        otl_busy_v = 0
        for T in range(nT):
            tok0 = 128 * T
            ir = T % 2
            bg = 4
            wait("tensor", s_dve, ffn_banks_v)
            self.bwait("tensor", 2)
            self.waitt("vector", idx_free[ir])
            wait("tensor", s_dve, routing_done_v)
            for u in range(8):
                ins("tensor", lambda t, u=u, T=T: t.matmul(
                    self.bank(2, 64 * u, 64)[0:32, :], foldc[:, u, :],
                    esel[:, T, :], start=True, stop=True),
                    s_mm if u == 7 else None)
            foldv = s_mm.n
            wait("vector", s_mm, foldv)
            ins("vector", lambda v, ir=ir: v.tensor_copy(
                out=idx_t[ir][0:32, :].rearrange(
                    "p (k u) -> p u k", u=8),
                in_=self.bank(2)[0:32, :].rearrange(
                    "p (u k) -> p u k", k=64)), s_dve)
            self.bset(2, s_dve)
            idxv = s_dve.n
            wait("gpsimd", s_dve, idxv)
            wait("tensor", s_dve, idxv)
            wait("sync", s_mm, otl_busy_v)
            olv = ld(otl1[:], OALL[T].rearrange("h p x -> p h x"),
                     sem=s_otl[0])

            def attn_dsg(dsg, T=T, ir=ir, olv=olv):
                nonlocal_state = None
                ab = dsg % 2
                self.wait("sync", s_mm, self.wo_busy_v)
                wv = 0
                for hc in range(NH):
                    wv = self.ld(wo_tile[:, hc, :],
                                 WoB[128 * hc:128 * hc + 128,
                                     512 * dsg:512 * dsg + 512], sem=s_wo)
                self.bwait("tensor", ab)
                wait("tensor", s_wo, wv)
                if dsg == 0:
                    wait("tensor", s_otl[0], olv)
                    if T == 0:
                        wait("tensor", s_dve, bop_v)
                for hc in range(NH):
                    ins("tensor", lambda t, hc=hc, ab=ab: t.matmul(
                        self.bank(ab), otl1[:, hc, :],
                        wo_tile[:, hc, :], start=(hc == 0), stop=False),
                        None)
                ins("tensor", lambda t, dsg=dsg, ab=ab: t.matmul(
                    self.bank(ab), onesb[:],
                    bop_row[:, 512 * dsg:512 * dsg + 512],
                    start=False, stop=True), s_mm)
                mmv = s_mm.n
                self.wo_busy_v = mmv
                wait("vector", s_mm, mmv)
                ins("vector", lambda v, ab=ab, ir=ir, dsg=dsg: v.tensor_copy(
                    out=att_t[ir][:, 512 * dsg:512 * dsg + 512],
                    in_=self.bank(ab)), s_dve)
                self.bset(ab, s_dve)
                return mmv
            cinfo = {}
            IC = GCALL // 16

            def consume(cc, bg=bg):
                gbc, gvc, siluv, gci = cinfo.pop(cc)
                sd0 = 2 * (gci % 4)
                wait("vector", s_act, siluv)
                wait("vector", s_mm, self.sdguard[gci % 4])
                for j in range(SLOTC):
                    slot = SLOTC * cc + j
                    ins("vector", lambda v, j=j, slot=slot, sd0=sd0:
                        v.tensor_scalar(
                            out=sdiag[sd0 + j][:], in0=eyeb[:],
                            scalar1=gcol[:, slot:slot + 1], scalar2=None,
                            op0=ALU.mult), s_dve)
                sdv = s_dve.n
                wait("tensor", s_dve, sdv)
                wait("tensor", s_gat2[gbc], gvc)
                for j in range(SLOTC):
                    slot = SLOTC * cc + j
                    for sg in range(4):
                        ins("tensor", lambda t, gbc=gbc, j=j, sg=sg, bg=bg,
                            slot=slot, sd0=sd0: t.matmul(
                                self.bank(bg + sg), sdiag[sd0 + j][:],
                                gbuf[gbc][:, j, D_PRIV + 512 * sg:
                                          D_PRIV + 512 * sg + 512],
                                start=(slot == 0), stop=(slot == 63)),
                            s_mm if (sg == 3 and j == SLOTC - 1) else None)
                gbuf_free[gbc] = (s_mm, s_mm.n)
                self.sdguard[gci % 4] = s_mm.n

            for c in range(nGC):
                gb = gcall % NGB
                self.waitt("gpsimd", gbuf_free[gb])
                wait("gpsimd", s_gat2[gb], last_gather_v[gb])
                ins("gpsimd", lambda g, gb=gb, c=c, ir=ir: g.dma_gather(
                    out_ap=gbuf[gb][:], in_ap=TBL[:],
                    idxs_ap=idx_t[ir][:, IC * c:IC * c + IC],
                    num_idxs=GCALL, num_idxs_reg=self._gregs["nidx"],
                    elem_size=TROW),
                    s_gat2[gb], 16)
                gv = s_gat2[gb].n
                last_gather_v[gb] = gv
                wait("vector", s_gat2[gb], gv)
                for j in range(SLOTC):
                    slot = SLOTC * c + j
                    ins("vector", lambda v, gb=gb, j=j, slot=slot, T=T:
                        v.scalar_tensor_tensor(
                            out=ztrash[(SLOTC * c + j) % 4][:, 0:1024],
                            in0=gbuf[gb][:, j, 0:D_PRIV], scalar=1.0,
                            in1=h_tm[:, T, :], op0=ALU.bypass, op1=ALU.mult,
                            accum_out=zcol[:, slot:slot + 1]),
                        s_dve if j == SLOTC - 1 else None)
                wait("vector", s_dve, s_dve.n)
                ins("vector", lambda v, c=c, T=T: v.tensor_tensor(
                    out=zw[:, SLOTC * c:SLOTC * c + SLOTC],
                    in0=zcol[:, SLOTC * c:SLOTC * c + SLOTC],
                    in1=gates[:, T, SLOTC * c:SLOTC * c + SLOTC],
                    op=ALU.mult), s_dve)
                zv = s_dve.n
                wait("scalar", s_dve, zv)
                ins("scalar", lambda sc, c=c: sc.activation(
                    out=gcol[:, SLOTC * c:SLOTC * c + SLOTC],
                    in_=zw[:, SLOTC * c:SLOTC * c + SLOTC], func=AF.Silu),
                    s_act)
                cinfo[c] = (gb, gv, s_act.n, gcall)
                if c >= 1:
                    consume(c - 1)
                if c % 7 == 5 and c // 7 < 4:
                    mmv = attn_dsg(c // 7)
                    if c // 7 == 3:
                        otl_busy_v = mmv
                gcall += 1
            consume(nGC - 1)
            idx_free[ir] = (s_gat2[gb], s_gat2[gb].n)
            ffn_mm_v = s_mm.n
            att_fill_v = s_dve.n
            if D_ATT is not None:
                wait("scalar", s_dve, att_fill_v)
                self.st(D_ATT[T], att_t[ir][:], sem=s_dbg)
            if D_Z is not None:
                ins("vector", lambda v: v.engine_nop(), s_dve)
                wait("scalar", s_dve, s_dve.n)
                self.st(D_Z[T], zcol[:], sem=s_dbg)
            self.waitt("sync", asm_free[ir])
            xv = ld(xin_t[ir][:], X[tok0:tok0 + 128, :], sem=s_asm[ir])
            wait("vector", s_mm, ffn_mm_v)
            wait("vector", s_asm[ir], s_asm[ir].n)
            if D_FFN is not None:
                ins("vector", lambda v, bg=bg, ir=ir: v.tensor_copy(
                    out=out_t[ir][:],
                    in_=self.psum[:, 512 * bg:512 * bg + 2048]), s_dve)
                wait("scalar", s_dve, s_dve.n)
                self.st(D_FFN[T], out_t[ir][:], sem=s_dbg)
                wait("vector", s_dbg, s_dbg.n)
            ins("vector", lambda v, ir=ir, bg=bg: v.tensor_tensor(
                out=out_t[ir][:],
                in0=self.psum[:, 512 * bg:512 * bg + 2048],
                in1=xin_t[ir][:], op=ALU.add), s_dve)
            ffn_banks_v = s_dve.n
            if st >= 4:
                wait("vector", s_dve, s_dve.n)
                ins("vector", lambda v, ir=ir: v.tensor_tensor(
                    out=out_t[ir][:], in0=out_t[ir][:], in1=att_t[ir][:],
                    op=ALU.add), s_dve)
            ov = s_dve.n
            ins("vector", lambda v: v.engine_nop(), s_dve)
            wait("scalar", s_dve, s_dve.n)
            sv = self.st(OUT[tok0:tok0 + 128, :], out_t[ir][:],
                         sem=outsems[ir])
            asm_free[ir] = (outsems[ir], sv)
        self.epilogue(OUT, out_t, write_dummy=False)

    def epilogue(self, OUT, out_t, write_dummy, extra_sems=()):
        ins, wait = self.ins, self.wait
        if write_dummy:
            for sm in extra_sems:
                wait("vector", sm, sm.n)
            ins("vector", lambda v: v.memset(out_t[0][:], 0.0), self.s_dve)
            wait("scalar", self.s_dve, self.s_dve.n)
            for T in range(nT):
                self.st(OUT[128 * T:128 * T + 128, :], out_t[0][:])
                wait("scalar", self.s_out, self.s_out.n)
        for sm in (self.s_out,) + tuple(extra_sems):
            wait("scalar", sm, sm.n)
            wait("sync", sm, sm.n)
            wait("gpsimd", sm, sm.n)

    def emit(self):
        nc = self.nc
        with nc.Block() as block:
            @block.sync
            def _(eng):
                for fn in self.ops["sync"]:
                    fn(eng)

            @block.gpsimd
            def _(eng):
                for fn in self.ops["gpsimd"]:
                    fn(eng)

            @block.tensor
            def _(eng):
                for fn in self.ops["tensor"]:
                    fn(eng)

            @block.vector
            def _(eng):
                for fn in self.ops["vector"]:
                    fn(eng)

            @block.scalar
            def _(eng):
                for fn in self.ops["scalar"]:
                    fn(eng)
        mybir.codegen_inst_isa_subclasses(nc)
        return nc


def build(stage=5, debug_outputs=()):
    mk = MK(stage=stage, debug_outputs=debug_outputs)
    return mk.emit(), mk


from concourse.bass_utils import run_bass_kernel_spmd


def kernel(**inputs):
    def a32(k):
        v = np.asarray(inputs[k])
        assert v.dtype == np.float32, (k, v.dtype)
        return np.ascontiguousarray(v)

    x = a32("L0_attn_x")
    consts = host_consts()
    base = {
        "wq": a32("L0_attn_Wq"), "bq": a32("L0_attn_bq"),
        "wk": a32("L0_attn_Wk"), "bk": a32("L0_attn_bk"),
        "wv": a32("L0_attn_Wv"), "bv": a32("L0_attn_bv"),
        "wo": a32("L0_attn_Wo"), "bo": a32("L0_attn_bo"),
        "wup": a32("L0_ffn_W_up"), "bup": a32("L0_ffn_b_up"),
        "wdn": a32("L0_ffn_W_down"), "bdn": a32("L0_ffn_b_down"),
        "wqr": a32("L0_ffn_W_q"), "keys": a32("L0_ffn_keys"),
        "de": a32("L0_ffn_down_embed"), "ue": a32("L0_ffn_up_embed"),
        **consts,
    }
    nc, _mk = build(stage=5)
    in_maps = [dict(base, x=x[c]) for c in range(8)]
    res = run_bass_kernel_spmd(nc, in_maps, core_ids=list(range(8)))
    out = np.stack([np.asarray(res.results[c]["out"]) for c in range(8)])
    return out.astype(np.float32)


# version 15
# speedup vs baseline: 1.0171x; 1.0171x over previous
import math
import numpy as np
import concourse.bass as bass
import concourse.mybir as mybir
from concourse.library_config import mlp as MLP_LIB

F32 = mybir.dt.float32
BF16 = mybir.dt.bfloat16
I16 = mybir.dt.int16
U32 = mybir.dt.uint32
U8 = mybir.dt.uint8
AF = mybir.ActivationFunctionType
ALU = mybir.AluOpType
AX = mybir.AxisListType

MASK_NEG = -30000.0
EXP_SHIFT = 4.0
INV_SQRT_HD = float(1.0 / np.sqrt(128.0))

S, D = 2048, 2048
NH = 16
D_CROSS, D_PRIV = 8192, 1024
N_EXPERTS = 16384
nT = S // 128
nDC = D // 128
nCC = D_CROSS // 128
nPC = D_PRIV // 128
CG = 4
GCALL = 256
SLOTC = 2
NGB = 5
nGC = 64 // SLOTC
TROW = D_PRIV + D


def host_consts():
    import ml_dtypes
    eye = np.eye(128, dtype=np.float32)
    ckq = np.where(np.arange(128)[None, :] >= np.arange(128)[:, None],
                   0.0, MASK_NEG).astype(np.float32)
    cqk = np.where(np.arange(128)[None, :] <= np.arange(128)[:, None],
                   0.0, MASK_NEG).astype(np.float32)
    fold = np.zeros((8, 128, 32), np.float32)
    for u in range(8):
        for p in range(32):
            fold[u, 16 * u + (p % 16), p] = 1.0
    return {
        "negs": np.full((128, 1), -EXP_SHIFT, np.float32),
        "foldc": fold,
        "eyef": eye,
        "eyeb": eye.astype(ml_dtypes.bfloat16),
        "caus_kq": ckq,
        "caus_qk": cqk,
        "onesf": np.ones((1, 128), np.float32),
        "onesb": np.ones((1, 128), np.float32).astype(ml_dtypes.bfloat16),
        "onesc": np.ones((128, 1), np.float32).astype(ml_dtypes.bfloat16),
    }


class Sem:
    def __init__(self, h):
        self.h = h
        self.n = 0


class MK:
    ENGINES = ("sync", "gpsimd", "tensor", "vector", "scalar")

    def __init__(self, stage=5, debug_outputs=()):
        self.stage = stage
        self.dbgnames = set(debug_outputs)
        self.nc = bass.Bass("TRN2", target_bir_lowering=False)
        self.ops = {e: [] for e in self.ENGINES}
        self.semcms = []
        self.build()

    def ins(self, engine, fn, sem=None, k=1):
        if sem is None:
            self.ops[engine].append(fn)
            return None
        sem.n += k
        self.ops[engine].append(
            lambda eng, fn=fn, s=sem, k=k: fn(eng).then_inc(s.h, k))
        return sem.n

    def wait(self, engine, sem, val=None):
        v = sem.n if val is None else val
        if v > 0:
            self.ops[engine].append(lambda eng, s=sem, v=v: eng.wait_ge(s.h, v))

    def waitt(self, engine, tup):
        s, v = tup
        if s is not None and v > 0:
            self.wait(engine, s, v)

    def mksem(self, name):
        cm = self.nc.semaphore(name)
        h = cm.__enter__()
        self.semcms.append(cm)
        return Sem(h)

    def ld(self, dst, src, sem=None, eng="sync"):
        sem = sem or self.s_x
        return self.ins(eng, lambda e, d=dst, s=src: e.dma_start(out=d, in_=s),
                        sem, 16)

    def st(self, dst, src, sem=None, eng="scalar"):
        sem = sem or self.s_out
        return self.ins(eng, lambda e, d=dst, s=src: e.dma_start(out=d, in_=s),
                        sem, 16)

    def ldT(self, dst, src, sem=None, eng="sync"):
        sem = sem or self.s_x
        return self.ins(eng, lambda e, d=dst, s=src: e.dma_start(
            out=d, in_=s, transpose=True), sem, 16)

    def bank(self, i, off=0, w=512):
        return self.psum[:, 512 * i + off:512 * i + off + w]

    def bwait(self, engine, i):
        self.waitt(engine, self.bank_free[i])

    def bset(self, i, sem):
        self.bank_free[i] = (sem, sem.n)

    def transpose_evac(self, src_ap, dst_ap, bank_i, ident, eng="vector"):
        p = src_ap.shape[0]
        f = src_ap.shape[-1]
        idt = ident if p == 128 else ident[0:p, 0:p]
        sem = self.s_dve if eng == "vector" else self.s_act
        self.bwait("tensor", bank_i)
        self.ins("tensor", lambda t, s=src_ap, b=bank_i, f=f, p=p, idt=idt:
                 t.transpose(out=self.bank(b, 0, p)[0:f, :], in_=s,
                             identity=idt), self.s_mm)
        mmv = self.s_mm.n
        self.wait(eng, self.s_mm, mmv)
        if eng == "vector":
            self.ins(eng, lambda v, d=dst_ap, b=bank_i, f=f, p=p:
                     v.tensor_copy(out=d, in_=self.bank(b, 0, p)[0:f, :]),
                     sem)
        else:
            self.ins(eng, lambda sc, d=dst_ap, b=bank_i, f=f, p=p:
                     sc.copy(out=d, in_=self.bank(b, 0, p)[0:f, :]), sem)
        self.bset(bank_i, sem)
        return sem.n

    def build(self):
        nc = self.nc
        st = self.stage
        ins, wait, ld, ldT = self.ins, self.wait, self.ld, self.ldT

        def di(name, shape, dt=F32):
            return nc.dram_tensor(name, shape, dt, kind="ExternalInput")
        X = di("x", [S, D])
        Wq = di("wq", [D, D]); Bq = di("bq", [D])
        Wk = di("wk", [D, D]); Bk = di("bk", [D])
        Wv = di("wv", [D, D]); Bv = di("bv", [D])
        Wo = di("wo", [D, D]); Bo = di("bo", [D])
        Wup = di("wup", [D_CROSS, D]); Bup = di("bup", [D_CROSS])
        Wdn = di("wdn", [D_PRIV, D_CROSS]); Bdn = di("bdn", [D_PRIV])
        Wqr = di("wqr", [1024, D_PRIV])
        KEYS = di("keys", [8, 128, 2, 64])
        DE = di("de", [N_EXPERTS, D_PRIV])
        UE = di("ue", [N_EXPERTS, D])
        EYEf = di("eyef", [128, 128])
        EYEb = di("eyeb", [128, 128], BF16)
        CAUS_KQ = di("caus_kq", [128, 128])
        CAUS_QK = di("caus_qk", [128, 128])
        ONESf = di("onesf", [1, 128])
        FOLDC = di("foldc", [8, 128, 32])
        NEGS = di("negs", [128, 1])
        ONESb = di("onesb", [1, 128], BF16)
        ONESc = di("onesc", [128, 1], BF16)
        OUT = nc.dram_tensor("out", [S, D], F32, kind="ExternalOutput")
        WupB = nc.dram_tensor("wupb", [D_CROSS, D], BF16)
        WdnB = nc.dram_tensor("wdnb", [D_PRIV, D_CROSS], BF16)
        WqrB = nc.dram_tensor("wqrb", [1024, D_PRIV], BF16)
        WoB = nc.dram_tensor("wob", [D, D], BF16)
        TBL = nc.dram_tensor("tbl", [N_EXPERTS, TROW], BF16)
        OALL = nc.dram_tensor("oall", [nT, NH, 128, 128], BF16)
        HTM = nc.dram_tensor("htm_s", [nT, 128, D_PRIV], BF16)
        ATTN = nc.dram_tensor("attn_s", [nT, 128, D], BF16)

        self.dbg = {}
        def dout(name, shape, dt=F32):
            if name in self.dbgnames:
                self.dbg[name] = nc.dram_tensor(
                    "dbg_" + name, shape, dt, kind="ExternalOutput")
            return self.dbg.get(name)
        D_XN = dout("xn", [nDC, 128, S], BF16)
        D_H = dout("h", [nPC, 128, S], BF16)
        D_HTM = dout("htm", [nT, 128, D_PRIV], BF16)
        D_Q = dout("q", [64, 16, 128], BF16)
        D_SIM = dout("sim", [nT, 128, 16, 128])
        D_S1V = dout("s1v", [nT, 128, 2, 8, 8])
        D_S1F = dout("s1f", [nT, 128, 2, 8, 8])
        D_ESEL = dout("esel", [128, nT, 64])
        D_GATES = dout("gates", [128, nT, 64])
        D_Z = dout("z", [nT, 128, 64])
        D_FFN = dout("ffn", [nT, 128, D])
        D_QH = dout("qh", [NH, 128, S], BF16)
        D_KH = dout("kh", [NH, 128, S], BF16)
        D_PT = dout("pt", [128, S], BF16)
        D_SUMS = dout("sums", [NH, 128, nT])
        D_ATT = dout("att", [nT, 128, D], BF16)

        def asb(name, shape, dt):
            return nc.alloc_sbuf_tensor(name, shape, dt)

        def aat(name, shape, dt, off):
            return nc.alloc_sbuf_tensor_at(name, shape, dt, offset=off)

        eyef = asb("eyef_s", [128, 128], F32)
        eyeb = asb("eyeb_s", [128, 128], BF16)
        caus_kq = asb("caus_kq_s", [128, 128], F32)
        caus_qk = asb("caus_qk_s", [128, 128], F32)
        onesf = asb("onesf_s", [1, 128], F32)
        foldc = asb("foldc_s", [128, 8, 32], F32)
        negs = asb("negs_s", [128, 1], F32)
        onesb = asb("onesb_s", [1, 128], BF16)
        onesc = asb("onesc_s", [128, 1], BF16)
        bias_up = asb("bias_up", [128, nCC], F32)
        bias_dn = asb("bias_dn", [128, nPC], F32)
        bias_q = asb("bias_q", [128, nDC], F32)
        bias_k = asb("bias_k", [128, nDC], F32)
        bv_sb = asb("bv_sb", [128, nDC], F32)
        bv_bb = asb("bv_bb", [128, nDC], BF16)
        stat = asb("stat", [128, 8], F32)
        stat2 = asb("stat2", [128, 16], F32)
        bias_stage = asb("bias_stage", [64, 128], F32)
        bstg_dn = asb("bstg_dn", [8, 128], F32)
        bstg_q = asb("bstg_q", [16, 128], F32)
        bstg_k = asb("bstg_k", [16, 128], F32)
        bstg_v = asb("bstg_v", [16, 128], F32)
        trash = asb("trash", [128, 1024], BF16)
        trash_b = asb("trash_b", [128, 1024], BF16)
        keysT = asb("keysT", [64, 16, 128], BF16)
        s1v = asb("s1v", [128, 2, 8, 8], F32)
        s1i = asb("s1i", [128, 2, 8, 8], U32)
        s1f = asb("s1f", [128, 2, 8, 8], F32)
        alls = asb("alls", [128, 8, 64], F32)
        scv = asb("scv", [128, 8, 8], F32)
        gexp = asb("gexp", [128, 8, 8], F32)
        sum8 = asb("sum8", [128, 8], F32)
        pk32 = asb("pk32", [128, 8, 8], U32)
        ku32 = asb("ku32", [128, 8, 8], U32)
        kxf = asb("kxf", [128, 8, 8], F32)
        kyf = asb("kyf", [128, 8, 8], F32)
        sel_t = asb("sel_t", [128, 8, 8], F32)
        sel_x = asb("sel_x", [128, 8, 8], F32)
        sel_y = asb("sel_y", [128, 8, 8], F32)
        msk8 = asb("msk8", [128, 8, 8], F32)
        esel = asb("esel", [128, nT, 64], F32)
        gates = asb("gates", [128, nT, 64], F32)
        sums_h = asb("sums_h", [128, nT], F32)
        sumsp = asb("sumsp", [128, 8], F32)
        recip_h = asb("recip_h", [128, nT], F32)
        idx_t = [asb(f"idx_t{i}", [128, 512], I16) for i in range(2)]
        zcol = asb("zcol", [128, 64], F32)
        zw = asb("zw", [128, 64], F32)
        gcol = asb("gcol", [128, 64], F32)

        ln_mean = asb("ln_mean", [128, 1], F32)
        ln_var = asb("ln_var", [128, 1], F32)
        ln_rstd = asb("ln_rstd", [128, 1], F32)
        atl = [asb(f"atl{i}", [128, 512], BF16) for i in range(2)]

        TOPB = 229120
        R_A = TOPB - 65536
        R_B = R_A - 65536
        R_C = R_B - 45056

        def overlay(name, shape, dt, off):
            return aat(name, shape, dt, off)

        xn = overlay("xn", [128, nDC, S], BF16, R_A)
        gbuf = [overlay(f"gbuf{i}", [128, SLOTC, TROW], BF16,
                        R_A + i * (SLOTC * TROW * 2)) for i in range(NGB)]
        h_f32 = overlay("h_f32", [128, nPC, S], F32, R_B)
        h_fm = overlay("h_fm", [128, nPC, S], BF16, R_B)
        h_tm = overlay("h_tm", [128, nT, D_PRIV], BF16, R_B + 32768)
        qh = overlay("qh", [128, S], BF16, R_B)
        kh = overlay("kh", [128, S], BF16, R_B + 4096)
        vfm = overlay("vfm", [128, S], F32, R_B + 8192)
        vtm = overlay("vtm", [128, nDC, 128], BF16, R_B + 16384)
        ptil = overlay("ptil", [128, S], BF16, R_B + 20480)
        wsl_pp = [overlay(f"wsl_pp{i}", [128, nDC, 128], BF16,
                          R_B + 24576 + i * 4096) for i in range(2)]
        sdiag = [overlay(f"sdiag{i}", [128, 128], BF16,
                         R_B + 16384 + i * 256) for i in range(8)]
        bo_row = overlay("bo_row", [1, D], F32, R_B)
        bo_rowb = overlay("bo_rowb", [1, D], BF16, R_B + 8192)
        bop_row = overlay("bop_row", [1, D], BF16, R_C + 40960)
        ztrash = [overlay(f"ztrash{i}", [128, 1024], BF16,
                          R_B + 24576 + i * 2048) for i in range(4)]
        wo_tile = overlay("wo_tile", [128, NH, 512], BF16, R_B)
        otl1 = overlay("otl1", [128, NH, 128], BF16, R_B + 18432)
        x_t = [overlay(f"x_t{i}", [128, D], F32, R_C + i * 8192)
               for i in range(2)]
        xn_tm = [overlay("xn_tm0", [128, D], F32, R_C + 16384),
                 overlay("xn_tm1", [128, D], F32, R_B)]
        wup_g = [overlay(f"wup_g{i}", [128, nDC, CG * 128], BF16,
                         R_C + i * 16384) for i in range(2)]
        wdn_g = overlay("wdn_g", [128, CG, D_PRIV], BF16, R_C + 32768)
        h1g = overlay("h1g", [128, CG, 512], BF16, R_C + 40960)
        wqr_t = overlay("wqr_t", [128, 16 * nPC, 64], BF16, R_C)
        q64 = overlay("q64", [64, 16, 128], BF16, R_C + 16384)
        sim_vb = overlay("sim_vb", [128, 16, 128], F32, R_C + 20480)
        keys_raw = overlay("keys_raw", [128, 16, 64], F32, R_C + 28672)
        htm_stage = [overlay(f"htm_stage{i}", [128, 1024], BF16,
                             R_C + 28672 + i * 2048) for i in range(2)]
        rrow = overlay("rrow", [1, 512], F32, R_C + 32768)
        rb_sb = overlay("rb_sb", [128, 512], F32, R_C + 34816)
        otile = [overlay(f"otile{i}", [128, 512], BF16,
                         R_C + 36864 + i * 1024) for i in range(2)]
        wo_cols = overlay("wo_cols", [128, NH, 512], BF16, R_C)
        otl = [overlay(f"otl{i}", [128, NH, 128], BF16, R_C + 16384 + i * 4096)
               for i in range(2)]
        att_t = [overlay(f"att_t{i}", [128, D], BF16, R_C + i * 4096)
                 for i in range(2)]
        xin_t = [overlay(f"xin_t{i}", [128, D], F32, R_C + 8192 + i * 8192)
                 for i in range(2)]
        out_t = [overlay(f"out_t{i}", [128, D], F32, R_C + 24576 + i * 8192)
                 for i in range(2)]

        guard = asb("alloc_guard", [128, 4], F32)
        gend = nc.lookup_mloc(guard).addr + 16
        assert gend <= R_C, f"smalls overflow into R_C: {gend} > {R_C}"

        self.psum = nc.alloc_psum_tensor("psum", [128, 4096], F32)
        self.bank_free = {i: (None, 0) for i in range(8)}

        self.s_x = s_x = self.mksem("s_x")
        self.s_mm = s_mm = self.mksem("s_mm")
        self.s_dve = s_dve = self.mksem("s_dve")
        self.s_act = s_act = self.mksem("s_act")
        self.s_out = s_out = self.mksem("s_out")
        s_outb = self.mksem("s_outb")
        s_xin = [self.mksem("s_xin0"), self.mksem("s_xin1")]
        s_cwup = self.mksem("s_cwup")
        s_cwdn = self.mksem("s_cwdn")
        s_cwqr = self.mksem("s_cwqr")
        s_cwo = self.mksem("s_cwo")
        s_ctbl = self.mksem("s_ctbl")
        s_wup = self.mksem("s_wup")
        s_wdn = self.mksem("s_wdn")
        s_wqr = self.mksem("s_wqr")
        s_wv2 = [self.mksem("s_wv0"), self.mksem("s_wv1")]
        s_wo = self.mksem("s_wo")
        s_otl = [self.mksem("s_otl0"), self.mksem("s_otl1")]
        s_ost = [self.mksem("s_ost0"), self.mksem("s_ost1")]
        s_ast = [self.mksem("s_ast0"), self.mksem("s_ast1")]
        s_gat2 = [self.mksem(f"s_gat{i}") for i in range(NGB)]
        s_asm = [self.mksem("s_asm0"), self.mksem("s_asm1")]
        s_idxL = [self.mksem("s_idxL0"), self.mksem("s_idxL1")]
        s_dbg = self.mksem("s_dbg")
        outsems = [s_out, s_outb]

        self._gregs = {}
        def _init_gp(g):
            g.load_library(MLP_LIB)
            self._gregs["nidx"] = g.to_reg(GCALL)
        ins("gpsimd", _init_gp)
        ld(eyef[:], EYEf[:])
        ld(eyeb[:], EYEb[:])
        ld(caus_kq[:], CAUS_KQ[:])
        ld(caus_qk[:], CAUS_QK[:])
        ld(onesf[:], ONESf[:])
        ld(foldc[:], FOLDC[:].rearrange("u t p -> t u p"))
        ld(negs[:], NEGS[:])
        ld(onesb[:], ONESb[:])
        ld(onesc[:], ONESc[:])
        bstage_v = ld(bias_stage[0:64, :],
                      Bup[:].rearrange("(c p) -> c p", p=128))
        ld(bstg_dn[:], Bdn[:].rearrange("(c p) -> c p", p=128))
        ld(bstg_q[:], Bq[:].rearrange("(c p) -> c p", p=128))
        ld(bstg_k[:], Bk[:].rearrange("(c p) -> c p", p=128))
        ld(bstg_v[:], Bv[:].rearrange("(c p) -> c p", p=128))
        consts_v = 0
        for pp in range(2):
            for hh in range(8):
                consts_v = ld(keys_raw[:, pp * 8 + hh, :], KEYS[hh, :, pp, :])

        def cast_dma(dst, src, pieces, sm):
            n = src.shape[0]
            stp = n // pieces
            for i in range(pieces):
                ins("gpsimd", lambda g, d=dst[i * stp:(i + 1) * stp],
                    s=src[i * stp:(i + 1) * stp]: g.dma_start(out=d, in_=s),
                    sm, 16)
            return sm.n

        wup_cast_v = cast_dma(WupB, Wup, 8, s_cwup)

        wait("tensor", s_x, consts_v)
        self.transpose_evac(bias_stage[0:64, :], bias_up[:], 6, eyef[:])
        self.transpose_evac(bstg_dn[:], bias_dn[:], 7, eyef[:])
        self.transpose_evac(bstg_q[:], bias_q[:], 6, eyef[:])
        self.transpose_evac(bstg_k[:], bias_k[:], 7, eyef[:])
        self.transpose_evac(bstg_v[:], bv_sb[:], 6, eyef[:])

        x_free_d = [0, 0]
        x_free_a = [0, 0]
        xn_tm_free = [0, 0]
        for T in range(nT):
            bi = T % 2
            b = 8 * bi
            wait("sync", s_dve, x_free_d[bi])
            wait("sync", s_act, x_free_a[bi])
            xv = ld(x_t[bi][:], X[128 * T:128 * T + 128, :], sem=s_xin[bi])
            xt = x_t[bi]
            wait("vector", s_xin[bi], xv)
            ins("vector", lambda v, xt=xt, b=b: v.reduce_sum(
                stat2[:, b + 0:b + 1], xt[:], axis=AX.X), s_dve)
            sum_v = s_dve.n
            wait("scalar", s_xin[bi], xv)
            ins("scalar", lambda sc, xt=xt, b=b: sc.activation(
                out=trash[:, 0:1024], in_=xt[:, 0:1024], func=AF.Square,
                accum_out=stat2[:, b + 1:b + 2]), s_act)
            ins("scalar", lambda sc, xt=xt, b=b: sc.activation(
                out=trash_b[:, 0:1024], in_=xt[:, 1024:2048],
                func=AF.Square,
                accum_out=stat2[:, b + 2:b + 3]), s_act)
            act_sq_v = s_act.n
            x_free_a[bi] = act_sq_v
            wait("vector", s_dve, sum_v)
            wait("vector", s_act, act_sq_v)
            ins("vector", lambda v, b=b: v.tensor_scalar(
                out=stat2[:, b + 3:b + 4], in0=stat2[:, b + 0:b + 1],
                scalar1=1.0 / D, scalar2=None, op0=ALU.mult))
            ins("vector", lambda v, b=b: v.tensor_tensor(
                out=stat2[:, b + 4:b + 5], in0=stat2[:, b + 1:b + 2],
                in1=stat2[:, b + 2:b + 3], op=ALU.add), s_dve)
            wait("vector", s_dve, s_dve.n)
            ins("vector", lambda v, b=b: v.tensor_tensor(
                out=stat2[:, b + 5:b + 6], in0=stat2[:, b + 3:b + 4],
                in1=stat2[:, b + 3:b + 4], op=ALU.mult), s_dve)
            wait("vector", s_dve, s_dve.n)
            ins("vector", lambda v, b=b: v.scalar_tensor_tensor(
                out=stat2[:, b + 6:b + 7], in0=stat2[:, b + 4:b + 5],
                scalar=1.0 / D, in1=stat2[:, b + 5:b + 6],
                op0=ALU.mult, op1=ALU.subtract), s_dve)
            wait("vector", s_dve, s_dve.n)
            ins("vector", lambda v, b=b: v.tensor_scalar(
                out=stat2[:, b + 6:b + 7], in0=stat2[:, b + 6:b + 7],
                scalar1=1e-5, scalar2=None, op0=ALU.add), s_dve)
            wait("vector", s_dve, s_dve.n)
            ins("vector", lambda v, b=b: v.reciprocal(
                out=stat2[:, b + 6:b + 7], in_=stat2[:, b + 6:b + 7]), s_dve)
            varpe_v = s_dve.n
            wait("scalar", s_dve, varpe_v)
            ins("scalar", lambda sc, b=b: sc.activation(
                out=stat2[:, b + 7:b + 8], in_=stat2[:, b + 6:b + 7],
                func=AF.Sqrt), s_act)
            rstd_v = s_act.n
            wait("vector", s_act, rstd_v)
            ins("vector", lambda v, b=b: v.tensor_tensor(
                out=stat2[:, b + 5:b + 6], in0=stat2[:, b + 3:b + 4],
                in1=stat2[:, b + 7:b + 8], op=ALU.mult), s_dve)
            wait("vector", s_dve, s_dve.n)
            wait("vector", s_mm, xn_tm_free[bi])
            ins("vector", lambda v, xt=xt, bi=bi, b=b: v.tensor_scalar(
                out=xn_tm[bi][:], in0=xt[:], scalar1=stat2[:, b + 7:b + 8],
                scalar2=stat2[:, b + 5:b + 6],
                op0=ALU.mult, op1=ALU.subtract), s_dve)
            big_v = s_dve.n
            x_free_d[bi] = big_v
            wait("tensor", s_dve, big_v)
            for j in range(nDC):
                self.transpose_evac(
                    xn_tm[bi][:, 128 * j:128 * j + 128],
                    xn[:, j, 128 * T:128 * T + 128], 6 + (j % 2), eyef[:],
                    eng="vector" if j % 2 == 0 else "scalar")
            xn_tm_free[bi] = s_mm.n
        xn_done_dve = s_dve.n
        xn_done_act = s_act.n
        wait("gpsimd", s_dve, xn_done_dve)
        wdn_cast_v = cast_dma(WdnB, Wdn, 4, s_cwdn)
        wqr_cast_v = cast_dma(WqrB, Wqr, 1, s_cwqr)
        wo_cast_v = cast_dma(WoB, Wo, 2, s_cwo) if st >= 4 else 0

        if D_XN is not None:
            wait("scalar", s_dve, xn_done_dve)
            wait("scalar", s_act, xn_done_act)
            for j in range(nDC):
                self.st(D_XN[j], xn[:, j, :], sem=s_dbg)

        if st < 2:
            self.epilogue(OUT, out_t, write_dummy=True,
                          extra_sems=(s_dbg,))
            return

        for sl in range(16):
            self.transpose_evac(keys_raw[:, sl, :], keysT[:, sl, :], 6,
                                eyef[:])
        keysT_v = s_dve.n

        wup_busy = [0, 0]
        wdn_busy_v = 0
        hacc_evac_v = 0
        h1_evac = {6: self.bank_free[6], 7: self.bank_free[7]}
        wait("sync", s_dve, keysT_v)
        wait("sync", s_act, xn_done_act)
        wait("tensor", s_dve, xn_done_dve)
        wait("tensor", s_act, xn_done_act)
        s_wup2 = [self.mksem("s_wup0"), self.mksem("s_wup1")]
        wup_seen = [0, 0]
        for cg in range(nCC // CG):
            gbn = cg % 2
            wait("sync", s_mm, wup_busy[gbn])
            wait("sync", s_cwup,
                 16 * ((128 * CG * (cg + 1) - 1) // (D_CROSS // 8) + 1))
            if cg == 0:
                wait("sync", s_cwdn, wdn_cast_v)
            wup_v = wdn_v = 0
            for d in range(nDC):
                wup_v = ldT(wup_g[gbn][:, d, :],
                            WupB[128 * CG * cg:128 * CG * (cg + 1),
                                 128 * d:128 * d + 128], sem=s_wup2[gbn])
            wait("sync", s_mm, wup_busy[1 - gbn])
            for ci in range(CG):
                c = cg * CG + ci
                wdn_v = ldT(wdn_g[:, ci, :],
                            WdnB[0:D_PRIV, 128 * c:128 * c + 128], sem=s_wdn)
            for sp in range(4):
                tok0 = 512 * sp
                wait("tensor", s_wup2[gbn], wup_v)
                wait("tensor", s_dve, hacc_evac_v)
                for ci in range(CG):
                    bk_i = 6 + (ci % 2)
                    self.waitt("tensor", h1_evac[bk_i])
                    for d in range(nDC):
                        ins("tensor", lambda t, ci=ci, d=d, b=bk_i, tok0=tok0,
                            gbn=gbn: t.matmul(
                                self.bank(b),
                                wup_g[gbn][:, d, 128 * ci:128 * ci + 128],
                                xn[:, d, tok0:tok0 + 512],
                                start=(d == 0), stop=(d == nDC - 1)),
                            s_mm if d == nDC - 1 else None)
                    mmv = s_mm.n
                    wait("scalar", s_mm, mmv)
                    wait("scalar", s_mm, wdn_busy_v)
                    ins("scalar", lambda sc, ci=ci, b=bk_i, c=cg * CG + ci:
                        sc.activation(out=h1g[:, ci, :], in_=self.bank(b),
                                      func=AF.Silu,
                                      bias=bias_up[:, c:c + 1]), s_act)
                    h1_evac[bk_i] = (s_act, s_act.n)
                h1gv = s_act.n
                wait("tensor", s_act, h1gv)
                wait("tensor", s_wdn, wdn_v)
                wait("tensor", s_dve, hacc_evac_v)
                for pc in range(nPC):
                    for ci in range(CG):
                        ins("tensor", lambda t, pc=pc, ci=ci: t.matmul(
                            self.bank(pc),
                            wdn_g[:, ci, 128 * pc:128 * pc + 128],
                            h1g[:, ci, :],
                            start=(ci == 0), stop=(ci == CG - 1)),
                            s_mm if ci == CG - 1 else None)
                wdn_busy_v = s_mm.n
                mmv = s_mm.n
                wait("vector", s_mm, mmv)
                for pc in range(nPC):
                    if cg == 0:
                        ins("vector", lambda v, pc=pc, tok0=tok0:
                            v.tensor_copy(
                                out=h_f32[:, pc, tok0:tok0 + 512],
                                in_=self.bank(pc)), s_dve)
                    else:
                        ins("vector", lambda v, pc=pc, tok0=tok0:
                            v.tensor_tensor(
                                out=h_f32[:, pc, tok0:tok0 + 512],
                                in0=h_f32[:, pc, tok0:tok0 + 512],
                                in1=self.bank(pc), op=ALU.add), s_dve)
                hacc_evac_v = s_dve.n
            wup_busy[gbn] = s_mm.n
        if st >= 3:
            wait("gpsimd", s_wup2[0], s_wup2[0].n)
            wait("gpsimd", s_wup2[1], s_wup2[1].n)
            wait("gpsimd", s_wdn, s_wdn.n)
            cast_dma(TBL[:, 0:D_PRIV], DE, 8, s_ctbl)
            cast_dma(TBL[:, D_PRIV:TROW], UE, 16, s_ctbl)
        tbl_cast_v = s_ctbl.n
        for i in range(8):
            self.bank_free[i] = (s_dve, hacc_evac_v)
        self.bank_free[6] = h1_evac[6]
        self.bank_free[7] = h1_evac[7]
        wait("vector", s_dve, s_dve.n)
        for pc in range(nPC):
            ins("vector", lambda v, pc=pc: v.tensor_scalar(
                out=h_f32[:, pc, :], in0=h_f32[:, pc, :],
                scalar1=bias_dn[:, pc:pc + 1], scalar2=None, op0=ALU.add),
                s_dve)
        hbias_v = s_dve.n
        wait("tensor", s_dve, hbias_v)
        hst_free = [(None, 0), (None, 0)]
        s_htm = self.mksem("s_htm")
        s_htr = [self.mksem("s_htr0"), self.mksem("s_htr1")]
        for T in range(nT):
            r = T % 2
            self.waitt("vector", hst_free[r])
            for pc in range(nPC):
                self.transpose_evac(h_f32[:, pc, 128 * T:128 * T + 128],
                                    htm_stage[r][:, 128 * pc:128 * pc + 128],
                                    6 + (pc % 2), eyef[:],
                                    eng="vector" if pc % 2 == 0 else "scalar")
            hv = s_dve.n
            wait("scalar", s_dve, hv)
            wait("scalar", s_act, s_act.n)
            sv = self.st(HTM[T], htm_stage[r][:], sem=s_htr[r])
            hst_free[r] = (s_htr[r], sv)
        htm_stored_v = [s_htr[0].n, s_htr[1].n]
        htr_mm_v = s_mm.n
        wait("vector", s_mm, htr_mm_v)
        ins("vector", lambda v: v.tensor_copy(
            out=trash[:, 0:1024], in_=h_f32[:, 0, 0:1024]))
        ins("vector", lambda v: v.tensor_copy(
            out=trash_b[:, 0:1024], in_=h_f32[:, 0, 1024:2048]), s_dve)
        wait("vector", s_dve, s_dve.n)
        ins("vector", lambda v: v.tensor_copy(
            out=h_fm[:, 0, 0:1024], in_=trash[:, 0:1024]))
        ins("vector", lambda v: v.tensor_copy(
            out=h_fm[:, 0, 1024:2048], in_=trash_b[:, 0:1024]), s_dve)
        wait("vector", s_dve, s_dve.n)
        for pc in range(1, nPC):
            ins("vector", lambda v, pc=pc: v.tensor_copy(
                out=h_fm[:, pc, :], in_=h_f32[:, pc, :]), s_dve)
            wait("vector", s_dve, s_dve.n)
        h_fm_v = s_dve.n
        wait("sync", s_htr[0], htm_stored_v[0])
        wait("sync", s_htr[1], htm_stored_v[1])
        wait("sync", s_dve, h_fm_v)
        htm_ld_v = ld(h_tm[:], HTM[:].rearrange("t p x -> p t x"),
                      sem=s_htm)
        h_tm_v = None
        if D_H is not None:
            wait("scalar", s_dve, h_fm_v)
            for pc in range(nPC):
                self.st(D_H[pc], h_fm[:, pc, :], sem=s_dbg)
        if D_HTM is not None:
            wait("scalar", s_htm, htm_ld_v)
            for T in range(nT):
                self.st(D_HTM[T], h_tm[:, T, :], sem=s_dbg)

        wait("sync", s_cwqr, wqr_cast_v)
        wait("sync", s_mm, max(wup_busy))
        wqr_v = 0
        for sl in range(16):
            for pc in range(nPC):
                wqr_v = ldT(wqr_t[:, sl * nPC + pc, :],
                            WqrB[64 * sl:64 * sl + 64,
                                 128 * pc:128 * pc + 128], sem=s_wqr)

        def insv(fn):
            ins("vector", fn, s_dve)
            wait("vector", s_dve, s_dve.n)

        routing_done_v = 0
        dbgq_v = 0
        for T in range(nT):
            tok0 = 128 * T
            wait("tensor", s_wqr, wqr_v)
            if dbgq_v:
                wait("vector", s_dbg, s_dbg.n)
            for sl in range(16):
                bk_i = 6 + (sl % 2)
                self.bwait("tensor", bk_i)
                for pc in range(nPC):
                    ins("tensor", lambda t, sl=sl, pc=pc, b=bk_i, tok0=tok0:
                        t.matmul(self.bank(b, 0, 128)[0:64, :],
                                 wqr_t[:, sl * nPC + pc, :],
                                 h_fm[:, pc, tok0:tok0 + 128],
                                 start=(pc == 0), stop=(pc == nPC - 1)),
                        s_mm if pc == nPC - 1 else None)
                mmv = s_mm.n
                wait("vector", s_mm, mmv)
                ins("vector", lambda v, sl=sl, b=bk_i: v.tensor_copy(
                    out=q64[:, sl, :], in_=self.bank(b, 0, 128)[0:64, :]),
                    s_dve)
                self.bset(bk_i, s_dve)
            q64_v = s_dve.n
            if D_Q is not None and T == 0:
                wait("scalar", s_dve, q64_v)
                self.st(D_Q[:], q64[:], sem=s_dbg)
                dbgq_v = s_dbg.n
            wait("tensor", s_dve, max(q64_v, keysT_v))
            for sl in range(16):
                bk_i = 6 + (sl % 2)
                self.bwait("tensor", bk_i)
                ins("tensor", lambda t, sl=sl, b=bk_i: t.matmul(
                    self.bank(b, 0, 128), q64[:, sl, :], keysT[:, sl, :],
                    start=True, stop=True), s_mm)
                mmv = s_mm.n
                wait("vector", s_mm, mmv)
                ins("vector", lambda v, sl=sl, b=bk_i: v.tensor_copy(
                    out=sim_vb[:, sl, :], in_=self.bank(b, 0, 128)), s_dve)
                self.bset(bk_i, s_dve)
            sim_v = s_dve.n
            if D_SIM is not None:
                wait("scalar", s_dve, sim_v)
                self.st(D_SIM[T], sim_vb[:], sem=s_dbg)
            wait("vector", s_dve, s_dve.n)
            for pp in range(2):
                for h in range(8):
                    ins("vector", lambda v, pp=pp, h=h: v.max(
                        out=s1v[:, pp, h, :], in_=sim_vb[:, pp * 8 + h, :]),
                        s_dve)
            wait("vector", s_dve, s_dve.n)
            for pp in range(2):
                for h in range(8):
                    ins("vector", lambda v, pp=pp, h=h: v.max_index(
                        out=s1i[:, pp, h, :], in_max=s1v[:, pp, h, :],
                        in_values=sim_vb[:, pp * 8 + h, :]), s_dve)
            wait("vector", s_dve, s_dve.n)
            insv(lambda v: v.tensor_copy(out=s1f[:], in_=s1i[:]))
            if D_S1V is not None:
                ins("vector", lambda v: v.engine_nop(), s_dve)
                wait("scalar", s_dve, s_dve.n)
                self.st(D_S1V[T], s1v[:], sem=s_dbg)
                self.st(D_S1F[T], s1f[:], sem=s_dbg)
            for h in range(8):
                ins("vector", lambda v, h=h: v.tensor_tensor(
                    out=alls[:, h, :].rearrange("p (a b) -> p a b", a=8),
                    in0=s1v[:, 0, h, :].rearrange("p (a o) -> p a o", o=1
                        ).to_broadcast([128, 8, 8]),
                    in1=s1v[:, 1, h, :].rearrange("p (o b) -> p o b", o=1
                        ).to_broadcast([128, 8, 8]),
                    op=ALU.add), s_dve)
            wait("vector", s_dve, s_dve.n)
            for h in range(8):
                ins("vector", lambda v, h=h: v.max(
                    out=scv[:, h, :], in_=alls[:, h, :]), s_dve)
            wait("vector", s_dve, s_dve.n)
            for h in range(8):
                ins("vector", lambda v, h=h: v.max_index(
                    out=pk32[:, h, :], in_max=scv[:, h, :],
                    in_values=alls[:, h, :]), s_dve)
            wait("vector", s_dve, s_dve.n)
            insv(lambda v: v.tensor_single_scalar(
                out=ku32[:], in_=pk32[:], scalar=3,
                op=ALU.logical_shift_right))
            insv(lambda v: v.tensor_copy(out=kxf[:], in_=ku32[:]))
            insv(lambda v: v.tensor_single_scalar(
                out=ku32[:], in_=pk32[:], scalar=7, op=ALU.bitwise_and))
            insv(lambda v: v.tensor_copy(out=kyf[:], in_=ku32[:]))
            for a in range(8):
                insv(lambda v, a=a: v.tensor_single_scalar(
                    out=msk8[:], in_=kxf[:], scalar=float(a),
                    op=ALU.is_equal))
                insv(lambda v, a=a: v.tensor_tensor(
                    out=sel_t[:], in0=msk8[:],
                    in1=s1f[:, 0, :, a].rearrange("p (h o) -> p h o", o=1
                        ).to_broadcast([128, 8, 8]),
                    op=ALU.mult))
                if a == 0:
                    insv(lambda v: v.tensor_copy(
                        out=sel_x[:], in_=sel_t[:]))
                else:
                    insv(lambda v: v.tensor_tensor(
                        out=sel_x[:], in0=sel_x[:], in1=sel_t[:], op=ALU.add))
                insv(lambda v, a=a: v.tensor_single_scalar(
                    out=msk8[:], in_=kyf[:], scalar=float(a),
                    op=ALU.is_equal))
                insv(lambda v, a=a: v.tensor_tensor(
                    out=sel_t[:], in0=msk8[:],
                    in1=s1f[:, 1, :, a].rearrange("p (h o) -> p h o", o=1
                        ).to_broadcast([128, 8, 8]),
                    op=ALU.mult))
                if a == 0:
                    insv(lambda v: v.tensor_copy(
                        out=sel_y[:], in_=sel_t[:]))
                else:
                    insv(lambda v: v.tensor_tensor(
                        out=sel_y[:], in0=sel_y[:], in1=sel_t[:], op=ALU.add))
            ins("vector", lambda v, T=T: v.scalar_tensor_tensor(
                out=esel[:, T, :].rearrange("p (h j) -> p h j", h=8),
                in0=sel_x[:], scalar=128.0, in1=sel_y[:],
                op0=ALU.mult, op1=ALU.add), s_dve)
            esel_v = s_dve.n
            wait("scalar", s_dve, esel_v)
            ins("scalar", lambda sc: sc.activation(
                out=gexp[:], in_=scv[:], func=AF.Exp), s_act)
            wait("vector", s_act, s_act.n)
            for h in range(8):
                ins("vector", lambda v, h=h: v.reduce_sum(
                    sum8[:, h:h + 1], gexp[:, h, :], axis=AX.X),
                    s_dve if h == 7 else None)
            wait("vector", s_dve, s_dve.n)
            ins("vector", lambda v: v.reciprocal(out=sum8[:], in_=sum8[:]),
                s_dve)
            wait("vector", s_dve, s_dve.n)
            ins("vector", lambda v, T=T: v.tensor_tensor(
                out=gates[:, T, :].rearrange("p (h j) -> p h j", h=8),
                in0=gexp[:],
                in1=sum8[:].rearrange("p (h o) -> p h o", o=1
                    ).to_broadcast([128, 8, 8]),
                op=ALU.mult), s_dve)
            routing_done_v = s_dve.n

        if D_ESEL is not None:
            wait("scalar", s_dve, routing_done_v)
            self.st(D_ESEL[:], esel[:], sem=s_dbg)
        if D_GATES is not None:
            wait("scalar", s_dve, routing_done_v)
            self.st(D_GATES[:], gates[:], sem=s_dbg)

        h_fm_last_use = s_mm.n
        xn_last_use = s_mm.n
        p35_psum_free = (None, 0)
        attn_done_v = 0

        if st >= 4:
            wsl_busy = [(None, 0), (None, 0)]
            wsl_load_v = [0, 0]
            qkv_banks_free = (None, 0)
            wait("gpsimd", s_mm, h_fm_last_use)
            projs = [(h, pj, W) for h in range(NH)
                     for pj, W in (("q", Wq), ("k", Wk), ("v", Wv))]

            def proj_load(i):
                h2, pj2, Wmat2 = projs[i]
                pb = i % 2
                self.waitt("gpsimd", wsl_busy[pb])
                wait("gpsimd", s_wv2[pb], wsl_load_v[pb])
                for d in range(nDC):
                    self.ins("gpsimd", lambda g, d=d, pb=pb, h2=h2,
                             Wmat2=Wmat2: g.dma_start(
                                 out=wsl_pp[pb][:, d, :],
                                 in_=Wmat2[128 * d:128 * d + 128,
                                           128 * h2:128 * h2 + 128]),
                             s_wv2[pb], 16)
                wsl_load_v[pb] = s_wv2[pb].n

            proj_load(0)
            for h in range(NH):
                for pjn, (proj, Wmat) in enumerate(
                        (("q", Wq), ("k", Wk), ("v", Wv))):
                    i = 3 * h + pjn
                    pb = i % 2
                    if i + 1 < len(projs):
                        proj_load(i + 1)
                    self.waitt("tensor", qkv_banks_free)
                    wait("tensor", s_wv2[pb], wsl_load_v[pb])
                    for d in range(nDC):
                        for sg in range(4):
                            ins("tensor", lambda t, d=d, sg=sg, pb=pb:
                                t.matmul(
                                    self.bank(sg), wsl_pp[pb][:, d, :],
                                    xn[:, d, 512 * sg:512 * sg + 512],
                                    start=(d == 0), stop=(d == nDC - 1)),
                                s_mm if (d == nDC - 1 and sg == 3) else None)
                    wsl_busy[pb] = (s_mm, s_mm.n)
                    xn_last_use = s_mm.n
                    mmv = s_mm.n
                    wait("scalar", s_mm, mmv)
                    if proj == "q":
                        ins("scalar", lambda sc, h=h: sc.activation(
                            out=qh[:], in_=self.psum[:, 0:2048],
                            func=AF.Identity, bias=bias_q[:, h:h + 1]),
                            s_act)
                    elif proj == "k":
                        ins("scalar", lambda sc, h=h: sc.activation(
                            out=kh[:], in_=self.psum[:, 0:2048],
                            func=AF.Identity, bias=bias_k[:, h:h + 1]),
                            s_act)
                    else:
                        ins("scalar", lambda sc: sc.copy(
                            out=vfm[:], in_=self.psum[:, 0:2048]), s_act)
                    qkv_banks_free = (s_act, s_act.n)
                qkv_v = s_act.n
                if D_QH is not None:
                    wait("scalar", s_act, qkv_v)
                    self.st(D_QH[h], qh[:], sem=s_dbg)
                    self.st(D_KH[h], kh[:], sem=s_dbg)
                wait("tensor", s_act, qkv_v)
                for kc in range(nDC):
                    self.transpose_evac(vfm[:, 128 * kc:128 * kc + 128],
                                        vtm[:, kc, :], 6 + (kc % 2), eyef[:])
                vtm_v = s_dve.n
                self.waitt("tensor", qkv_banks_free)
                for sg2 in range(4):
                    sb_i = 6 + (sg2 % 2)
                    self.bwait("tensor", sb_i)
                    self.bwait("tensor", sg2)
                    nkc = 4 * sg2 + 4

                    def a2(kc, sg2=sg2):
                        off = max(0, 128 * kc - 512 * sg2)
                        w = 512 - off
                        bk_i = 4 + (kc % 2)
                        self.bwait("tensor", bk_i)
                        ins("tensor", lambda t, kc=kc, sg2=sg2, off=off, w=w,
                            b=bk_i: t.matmul(
                                self.bank(b, off, w),
                                kh[:, 128 * kc:128 * kc + 128],
                                qh[:, 512 * sg2 + off:512 * sg2 + off + w],
                                start=True, stop=True), s_mm)
                        mmv = s_mm.n
                        if kc >= 4 * sg2:
                            wait("vector", s_mm, mmv)
                            ins("vector", lambda v, b=bk_i, off=off:
                                v.tensor_tensor(
                                    out=self.bank(b, off, 128),
                                    in0=self.bank(b, off, 128),
                                    in1=caus_kq[:], op=ALU.add), s_dve)
                            wait("scalar", s_dve, s_dve.n)
                        else:
                            wait("scalar", s_mm, mmv)
                        ins("scalar", lambda sc, b=bk_i, off=off, w=w,
                            kc=kc: sc.activation(
                                out=ptil[:, 512 * (kc % 2) + off:
                                         512 * (kc % 2) + off + w],
                                in_=self.bank(b, off, w), func=AF.Exp,
                                scale=INV_SQRT_HD, bias=negs[:]), s_act)
                        self.bset(bk_i, s_act)
                        return s_act.n

                    ptv = a2(0)
                    for kc in range(nkc):
                        off = max(0, 128 * kc - 512 * sg2)
                        w = 512 - off
                        nxt = a2(kc + 1) if kc + 1 < nkc else None
                        wait("tensor", s_act, ptv)
                        if kc == 0 and sg2 == 0:
                            wait("tensor", s_dve, vtm_v)
                        last = (kc == nkc - 1)
                        ins("tensor", lambda t, kc=kc, sg2=sg2, off=off, w=w,
                            last=last: t.matmul(
                                self.bank(sg2, off, w), vtm[:, kc, :],
                                ptil[:, 512 * (kc % 2) + off:
                                     512 * (kc % 2) + off + w],
                                start=(kc == 0), stop=last),
                            s_mm if last else None)
                        ins("tensor", lambda t, kc=kc, off=off, w=w,
                            b=sb_i, last=last: t.matmul(
                                self.bank(b, off, w)[0:1, :], onesc[:],
                                ptil[:, 512 * (kc % 2) + off:
                                     512 * (kc % 2) + off + w],
                                start=(kc == 0), stop=last),
                            s_mm if last else None)
                        ptv = nxt
                    av_done = s_mm.n
                    wait("vector", s_mm, av_done)
                    ins("vector", lambda v, b=sb_i: v.tensor_copy(
                        out=rrow[:], in_=self.bank(b, 0, 512)[0:1, :]),
                        s_dve)
                    wait("vector", s_dve, s_dve.n)
                    ins("vector", lambda v: v.reciprocal(
                        out=rrow[:], in_=rrow[:]), s_dve)
                    rrv = s_dve.n
                    self.bset(sb_i, s_dve)
                    self.bwait("tensor", sb_i)
                    wait("tensor", s_dve, rrv)
                    ins("tensor", lambda t, b=sb_i: t.matmul(
                        self.bank(b), onesf[:], rrow[:],
                        start=True, stop=True), s_mm)
                    mmv = s_mm.n
                    wait("vector", s_mm, mmv)
                    ins("vector", lambda v, b=sb_i: v.tensor_copy(
                        out=rb_sb[:], in_=self.bank(b)), s_dve)
                    self.bset(sb_i, s_dve)
                    wait("vector", s_dve, s_dve.n)
                    ot = otile[sg2 % 2]
                    if hasattr(self, "ot_store"):
                        wait("vector", s_ost[sg2 % 2],
                             self.ot_store[sg2 % 2])
                    ins("vector", lambda v, sg2=sg2, ot=ot:
                        v.scalar_tensor_tensor(
                            out=ot[:],
                            in0=self.bank(sg2, 0, 512),
                            scalar=1.0, in1=rb_sb[:],
                            op0=ALU.bypass, op1=ALU.mult), s_dve)
                    self.bset(sg2, s_dve)
                    ov = s_dve.n
                    wait("sync", s_dve, ov)
                    sv = self.st(
                        OALL[4 * sg2:4 * sg2 + 4, h].rearrange(
                            "t p x -> p t x"),
                        ot[:].rearrange("p (t x) -> p t x", t=4),
                        sem=s_ost[sg2 % 2], eng="sync")
                    if not hasattr(self, "ot_store"):
                        self.ot_store = [0, 0]
                    self.ot_store[sg2 % 2] = sv
                qkv_banks_free = (s_dve, s_dve.n)

            wait("sync", s_cwo, wo_cast_v)
            wait("sync", s_act, s_act.n)
            wait("sync", s_mm, s_mm.n)
            s_bol = self.mksem("s_bol")
            bol_v = ld(bo_row[:], Bo[:].rearrange("(o d) -> o d", o=1),
                       sem=s_bol)
            wait("vector", s_bol, bol_v)
            ins("vector", lambda v: v.tensor_copy(
                out=bo_rowb[:], in_=bo_row[:]), s_dve)
            ins("vector", lambda v: v.tensor_copy(
                out=bv_bb[:], in_=bv_sb[:]), s_dve)
            borow_v = s_dve.n
            wo_v = 0
            wait("sync", s_mm, s_mm.n)
            wo_first_v = None
            p35_bank_free = [(s_act, s_act.n), (s_act, s_act.n)]
            att_sv = [0, 0]
            oall_done_v = [s_ost[0].n, s_ost[1].n]
            for dsg in range(4):
                wait("sync", s_mm, s_mm.n if dsg > 0 else 0)
                for hc in range(NH):
                    wo_v = ld(wo_cols[:, hc, :],
                              WoB[128 * hc:128 * hc + 128,
                                  512 * dsg:512 * dsg + 512], sem=s_wo)
                self.bwait("tensor", 6)
                wait("tensor", s_wo, wo_v)
                if dsg == 0:
                    wait("tensor", s_dve, borow_v)
                for hc in range(NH):
                    ins("tensor", lambda t, hc=hc: t.matmul(
                        self.bank(6)[0:1, :], bv_bb[:, hc:hc + 1],
                        wo_cols[:, hc, :], start=(hc == 0), stop=False), None)
                ins("tensor", lambda t, dsg=dsg: t.matmul(
                    self.bank(6)[0:1, :], onesb[0:1, 0:1],
                    bo_rowb[:, 512 * dsg:512 * dsg + 512],
                    start=False, stop=True), s_mm)
                mmv = s_mm.n
                wait("vector", s_mm, mmv)
                ins("vector", lambda v, dsg=dsg: v.tensor_copy(
                    out=bop_row[:, 512 * dsg:512 * dsg + 512],
                    in_=self.bank(6)[0:1, :]), s_dve)
                self.bset(6, s_dve)
                bop_v = s_dve.n
            wait("sync", s_ost[0], oall_done_v[0])
            wait("sync", s_ost[1], oall_done_v[1])
            p35_psum_free = (s_dve, s_dve.n)

        if st < 3:
            self.epilogue(OUT, out_t, write_dummy=True,
                          extra_sems=(s_dbg,))
            return

        self._dbgh = dict(TBL=TBL, idx_t=idx_t, gbuf=gbuf)
        wait("gpsimd", s_mm, xn_last_use)
        wait("gpsimd", s_ctbl, tbl_cast_v)
        wait("gpsimd", s_cwup, s_cwup.n)
        wait("gpsimd", s_cwdn, s_cwdn.n)
        wait("gpsimd", s_cwqr, s_cwqr.n)
        if st >= 4:
            wait("gpsimd", s_cwo, s_cwo.n)
        wait("vector", s_htm, htm_ld_v)
        ins("vector", lambda v: v.memset(idx_t[0][:], 0))
        ins("vector", lambda v: v.memset(idx_t[1][:], 0), s_dve)
        wait("vector", s_dve, s_dve.n)
        wait("sync", s_mm, s_mm.n)
        wait("sync", s_dve, s_dve.n)
        wait("sync", s_act, s_act.n)
        wait("sync", s_dbg, s_dbg.n)
        self.waitt("tensor", p35_psum_free)
        idx_free = [(None, 0), (None, 0)]
        gbuf_free = [(None, 0) for _ in range(NGB)]
        ffn_banks_v = 0
        asm_free = [(None, 0), (None, 0)]
        last_gather_v = [0] * NGB
        self.sdguard = [0, 0, 0, 0]
        gcall = 0
        self.wo_busy_v = 0
        otl_busy_v = 0
        for T in range(nT):
            tok0 = 128 * T
            ir = T % 2
            bg = 4
            wait("tensor", s_dve, ffn_banks_v)
            self.bwait("tensor", 2)
            self.waitt("vector", idx_free[ir])
            wait("tensor", s_dve, routing_done_v)
            for u in range(8):
                ins("tensor", lambda t, u=u, T=T: t.matmul(
                    self.bank(2, 64 * u, 64)[0:32, :], foldc[:, u, :],
                    esel[:, T, :], start=True, stop=True),
                    s_mm if u == 7 else None)
            foldv = s_mm.n
            wait("vector", s_mm, foldv)
            ins("vector", lambda v, ir=ir: v.tensor_copy(
                out=idx_t[ir][0:32, :].rearrange(
                    "p (k u) -> p u k", u=8),
                in_=self.bank(2)[0:32, :].rearrange(
                    "p (u k) -> p u k", k=64)), s_dve)
            self.bset(2, s_dve)
            idxv = s_dve.n
            wait("gpsimd", s_dve, idxv)
            wait("tensor", s_dve, idxv)
            wait("sync", s_mm, otl_busy_v)
            olv = ld(otl1[:], OALL[T].rearrange("h p x -> p h x"),
                     sem=s_otl[0])

            def attn_dsg(dsg, T=T, ir=ir, olv=olv):
                nonlocal_state = None
                ab = dsg % 2
                self.wait("sync", s_mm, self.wo_busy_v)
                wv = 0
                for hc in range(NH):
                    wv = self.ld(wo_tile[:, hc, :],
                                 WoB[128 * hc:128 * hc + 128,
                                     512 * dsg:512 * dsg + 512], sem=s_wo)
                self.bwait("tensor", ab)
                wait("tensor", s_wo, wv)
                if dsg == 0:
                    wait("tensor", s_otl[0], olv)
                    if T == 0:
                        wait("tensor", s_dve, bop_v)
                for hc in range(NH):
                    ins("tensor", lambda t, hc=hc, ab=ab: t.matmul(
                        self.bank(ab), otl1[:, hc, :],
                        wo_tile[:, hc, :], start=(hc == 0), stop=False),
                        None)
                ins("tensor", lambda t, dsg=dsg, ab=ab: t.matmul(
                    self.bank(ab), onesb[:],
                    bop_row[:, 512 * dsg:512 * dsg + 512],
                    start=False, stop=True), s_mm)
                mmv = s_mm.n
                self.wo_busy_v = mmv
                wait("vector", s_mm, mmv)
                ins("vector", lambda v, ab=ab, ir=ir, dsg=dsg: v.tensor_copy(
                    out=att_t[ir][:, 512 * dsg:512 * dsg + 512],
                    in_=self.bank(ab)), s_dve)
                self.bset(ab, s_dve)
                return mmv
            cinfo = {}
            IC = GCALL // 16

            def consume(cc, bg=bg):
                gbc, gvc, siluv, gci = cinfo.pop(cc)
                sd0 = 2 * (gci % 4)
                wait("vector", s_act, siluv)
                wait("vector", s_mm, self.sdguard[gci % 4])
                for j in range(SLOTC):
                    slot = SLOTC * cc + j
                    ins("vector", lambda v, j=j, slot=slot, sd0=sd0:
                        v.tensor_scalar(
                            out=sdiag[sd0 + j][:], in0=eyeb[:],
                            scalar1=gcol[:, slot:slot + 1], scalar2=None,
                            op0=ALU.mult), s_dve)
                sdv = s_dve.n
                wait("tensor", s_dve, sdv)
                wait("tensor", s_gat2[gbc], gvc)
                for j in range(SLOTC):
                    slot = SLOTC * cc + j
                    for sg in range(4):
                        ins("tensor", lambda t, gbc=gbc, j=j, sg=sg, bg=bg,
                            slot=slot, sd0=sd0: t.matmul(
                                self.bank(bg + sg), sdiag[sd0 + j][:],
                                gbuf[gbc][:, j, D_PRIV + 512 * sg:
                                          D_PRIV + 512 * sg + 512],
                                start=(slot == 0), stop=(slot == 63)),
                            s_mm if (sg == 3 and j == SLOTC - 1) else None)
                gbuf_free[gbc] = (s_mm, s_mm.n)
                self.sdguard[gci % 4] = s_mm.n

            for c in range(nGC):
                gb = gcall % NGB
                self.waitt("gpsimd", gbuf_free[gb])
                wait("gpsimd", s_gat2[gb], last_gather_v[gb])
                ins("gpsimd", lambda g, gb=gb, c=c, ir=ir: g.dma_gather(
                    out_ap=gbuf[gb][:], in_ap=TBL[:],
                    idxs_ap=idx_t[ir][:, IC * c:IC * c + IC],
                    num_idxs=GCALL, num_idxs_reg=self._gregs["nidx"],
                    elem_size=TROW),
                    s_gat2[gb], 16)
                gv = s_gat2[gb].n
                last_gather_v[gb] = gv
                wait("vector", s_gat2[gb], gv)
                for j in range(SLOTC):
                    slot = SLOTC * c + j
                    ins("vector", lambda v, gb=gb, j=j, slot=slot, T=T:
                        v.scalar_tensor_tensor(
                            out=ztrash[(SLOTC * c + j) % 4][:, 0:1024],
                            in0=gbuf[gb][:, j, 0:D_PRIV], scalar=1.0,
                            in1=h_tm[:, T, :], op0=ALU.bypass, op1=ALU.mult,
                            accum_out=zcol[:, slot:slot + 1]),
                        s_dve if j == SLOTC - 1 else None)
                wait("vector", s_dve, s_dve.n)
                ins("vector", lambda v, c=c, T=T: v.tensor_tensor(
                    out=zw[:, SLOTC * c:SLOTC * c + SLOTC],
                    in0=zcol[:, SLOTC * c:SLOTC * c + SLOTC],
                    in1=gates[:, T, SLOTC * c:SLOTC * c + SLOTC],
                    op=ALU.mult), s_dve)
                zv = s_dve.n
                wait("scalar", s_dve, zv)
                ins("scalar", lambda sc, c=c: sc.activation(
                    out=gcol[:, SLOTC * c:SLOTC * c + SLOTC],
                    in_=zw[:, SLOTC * c:SLOTC * c + SLOTC], func=AF.Silu),
                    s_act)
                cinfo[c] = (gb, gv, s_act.n, gcall)
                if c >= 1:
                    consume(c - 1)
                if c % 7 == 5 and c // 7 < 4:
                    mmv = attn_dsg(c // 7)
                    if c // 7 == 3:
                        otl_busy_v = mmv
                gcall += 1
            consume(nGC - 1)
            idx_free[ir] = (s_gat2[gb], s_gat2[gb].n)
            ffn_mm_v = s_mm.n
            att_fill_v = s_dve.n
            if D_ATT is not None:
                wait("scalar", s_dve, att_fill_v)
                self.st(D_ATT[T], att_t[ir][:], sem=s_dbg)
            if D_Z is not None:
                ins("vector", lambda v: v.engine_nop(), s_dve)
                wait("scalar", s_dve, s_dve.n)
                self.st(D_Z[T], zcol[:], sem=s_dbg)
            self.waitt("sync", asm_free[ir])
            xv = ld(xin_t[ir][:], X[tok0:tok0 + 128, :], sem=s_asm[ir])
            wait("vector", s_mm, ffn_mm_v)
            wait("vector", s_asm[ir], s_asm[ir].n)
            if D_FFN is not None:
                ins("vector", lambda v, bg=bg, ir=ir: v.tensor_copy(
                    out=out_t[ir][:],
                    in_=self.psum[:, 512 * bg:512 * bg + 2048]), s_dve)
                wait("scalar", s_dve, s_dve.n)
                self.st(D_FFN[T], out_t[ir][:], sem=s_dbg)
                wait("vector", s_dbg, s_dbg.n)
            ins("vector", lambda v, ir=ir, bg=bg: v.tensor_tensor(
                out=out_t[ir][:],
                in0=self.psum[:, 512 * bg:512 * bg + 2048],
                in1=xin_t[ir][:], op=ALU.add), s_dve)
            ffn_banks_v = s_dve.n
            if st >= 4:
                wait("vector", s_dve, s_dve.n)
                ins("vector", lambda v, ir=ir: v.tensor_tensor(
                    out=out_t[ir][:], in0=out_t[ir][:], in1=att_t[ir][:],
                    op=ALU.add), s_dve)
            ov = s_dve.n
            ins("vector", lambda v: v.engine_nop(), s_dve)
            wait("scalar", s_dve, s_dve.n)
            sv = self.st(OUT[tok0:tok0 + 128, :], out_t[ir][:],
                         sem=outsems[ir])
            asm_free[ir] = (outsems[ir], sv)
        self.epilogue(OUT, out_t, write_dummy=False)

    def epilogue(self, OUT, out_t, write_dummy, extra_sems=()):
        ins, wait = self.ins, self.wait
        if write_dummy:
            for sm in extra_sems:
                wait("vector", sm, sm.n)
            ins("vector", lambda v: v.memset(out_t[0][:], 0.0), self.s_dve)
            wait("scalar", self.s_dve, self.s_dve.n)
            for T in range(nT):
                self.st(OUT[128 * T:128 * T + 128, :], out_t[0][:])
                wait("scalar", self.s_out, self.s_out.n)
        for sm in (self.s_out,) + tuple(extra_sems):
            wait("scalar", sm, sm.n)
            wait("sync", sm, sm.n)
            wait("gpsimd", sm, sm.n)

    def emit(self):
        nc = self.nc
        with nc.Block() as block:
            @block.sync
            def _(eng):
                for fn in self.ops["sync"]:
                    fn(eng)

            @block.gpsimd
            def _(eng):
                for fn in self.ops["gpsimd"]:
                    fn(eng)

            @block.tensor
            def _(eng):
                for fn in self.ops["tensor"]:
                    fn(eng)

            @block.vector
            def _(eng):
                for fn in self.ops["vector"]:
                    fn(eng)

            @block.scalar
            def _(eng):
                for fn in self.ops["scalar"]:
                    fn(eng)
        mybir.codegen_inst_isa_subclasses(nc)
        return nc


def build(stage=5, debug_outputs=()):
    mk = MK(stage=stage, debug_outputs=debug_outputs)
    return mk.emit(), mk


from concourse.bass_utils import run_bass_kernel_spmd


def kernel(**inputs):
    def a32(k):
        v = np.asarray(inputs[k])
        assert v.dtype == np.float32, (k, v.dtype)
        return np.ascontiguousarray(v)

    x = a32("L0_attn_x")
    consts = host_consts()
    base = {
        "wq": a32("L0_attn_Wq"), "bq": a32("L0_attn_bq"),
        "wk": a32("L0_attn_Wk"), "bk": a32("L0_attn_bk"),
        "wv": a32("L0_attn_Wv"), "bv": a32("L0_attn_bv"),
        "wo": a32("L0_attn_Wo"), "bo": a32("L0_attn_bo"),
        "wup": a32("L0_ffn_W_up"), "bup": a32("L0_ffn_b_up"),
        "wdn": a32("L0_ffn_W_down"), "bdn": a32("L0_ffn_b_down"),
        "wqr": a32("L0_ffn_W_q"), "keys": a32("L0_ffn_keys"),
        "de": a32("L0_ffn_down_embed"), "ue": a32("L0_ffn_up_embed"),
        **consts,
    }
    nc, _mk = build(stage=5)
    in_maps = [dict(base, x=x[c]) for c in range(8)]
    res = run_bass_kernel_spmd(nc, in_maps, core_ids=list(range(8)))
    out = np.stack([np.asarray(res.results[c]["out"]) for c in range(8)])
    return out.astype(np.float32)


# version 16 (final)
# speedup vs baseline: 1.0559x; 1.0382x over previous
import math
import numpy as np
import concourse.bass as bass
import concourse.mybir as mybir
from concourse.library_config import mlp as MLP_LIB

F32 = mybir.dt.float32
BF16 = mybir.dt.bfloat16
I16 = mybir.dt.int16
U32 = mybir.dt.uint32
U8 = mybir.dt.uint8
AF = mybir.ActivationFunctionType
ALU = mybir.AluOpType
AX = mybir.AxisListType

MASK_NEG = -30000.0
EXP_SHIFT = 4.0
INV_SQRT_HD = float(1.0 / np.sqrt(128.0))

S, D = 2048, 2048
NH = 16
D_CROSS, D_PRIV = 8192, 1024
N_EXPERTS = 16384
nT = S // 128
nDC = D // 128
nCC = D_CROSS // 128
nPC = D_PRIV // 128
CG = 4
GCALL = 256
SLOTC = 2
NGB = 5
nGC = 64 // SLOTC
TROW = D_PRIV + D


def host_consts():
    import ml_dtypes
    eye = np.eye(128, dtype=np.float32)
    ckq = np.where(np.arange(128)[None, :] >= np.arange(128)[:, None],
                   0.0, MASK_NEG).astype(np.float32)
    cqk = np.where(np.arange(128)[None, :] <= np.arange(128)[:, None],
                   0.0, MASK_NEG).astype(np.float32)
    fold = np.zeros((8, 128, 32), np.float32)
    for u in range(8):
        for p in range(32):
            fold[u, 16 * u + (p % 16), p] = 1.0
    return {
        "negs": np.full((128, 1), -EXP_SHIFT, np.float32),
        "foldc": fold,
        "eyef": eye,
        "eyeb": eye.astype(ml_dtypes.bfloat16),
        "caus_kq": ckq,
        "caus_qk": cqk,
        "onesf": np.ones((1, 128), np.float32),
        "onesb": np.ones((1, 128), np.float32).astype(ml_dtypes.bfloat16),
        "onesc": np.ones((128, 1), np.float32).astype(ml_dtypes.bfloat16),
    }


class Sem:
    def __init__(self, h):
        self.h = h
        self.n = 0


class MK:
    ENGINES = ("sync", "gpsimd", "tensor", "vector", "scalar")

    def __init__(self, stage=5, debug_outputs=()):
        self.stage = stage
        self.dbgnames = set(debug_outputs)
        self.nc = bass.Bass("TRN2", target_bir_lowering=False)
        self.ops = {e: [] for e in self.ENGINES}
        self.semcms = []
        self.build()

    def ins(self, engine, fn, sem=None, k=1):
        if sem is None:
            self.ops[engine].append(fn)
            return None
        sem.n += k
        self.ops[engine].append(
            lambda eng, fn=fn, s=sem, k=k: fn(eng).then_inc(s.h, k))
        return sem.n

    def wait(self, engine, sem, val=None):
        v = sem.n if val is None else val
        if v > 0:
            self.ops[engine].append(lambda eng, s=sem, v=v: eng.wait_ge(s.h, v))

    def waitt(self, engine, tup):
        s, v = tup
        if s is not None and v > 0:
            self.wait(engine, s, v)

    def mksem(self, name):
        cm = self.nc.semaphore(name)
        h = cm.__enter__()
        self.semcms.append(cm)
        return Sem(h)

    def ld(self, dst, src, sem=None, eng="sync"):
        sem = sem or self.s_x
        return self.ins(eng, lambda e, d=dst, s=src: e.dma_start(out=d, in_=s),
                        sem, 16)

    def st(self, dst, src, sem=None, eng="scalar"):
        sem = sem or self.s_out
        return self.ins(eng, lambda e, d=dst, s=src: e.dma_start(out=d, in_=s),
                        sem, 16)

    def ldT(self, dst, src, sem=None, eng="sync"):
        sem = sem or self.s_x
        return self.ins(eng, lambda e, d=dst, s=src: e.dma_start(
            out=d, in_=s, transpose=True), sem, 16)

    def bank(self, i, off=0, w=512):
        return self.psum[:, 512 * i + off:512 * i + off + w]

    def bwait(self, engine, i):
        self.waitt(engine, self.bank_free[i])

    def bset(self, i, sem):
        self.bank_free[i] = (sem, sem.n)

    def transpose_evac(self, src_ap, dst_ap, bank_i, ident, eng="vector"):
        p = src_ap.shape[0]
        f = src_ap.shape[-1]
        idt = ident if p == 128 else ident[0:p, 0:p]
        sem = self.s_dve if eng == "vector" else self.s_act
        self.bwait("tensor", bank_i)
        self.ins("tensor", lambda t, s=src_ap, b=bank_i, f=f, p=p, idt=idt:
                 t.transpose(out=self.bank(b, 0, p)[0:f, :], in_=s,
                             identity=idt), self.s_mm)
        mmv = self.s_mm.n
        self.wait(eng, self.s_mm, mmv)
        if eng == "vector":
            self.ins(eng, lambda v, d=dst_ap, b=bank_i, f=f, p=p:
                     v.tensor_copy(out=d, in_=self.bank(b, 0, p)[0:f, :]),
                     sem)
        else:
            self.ins(eng, lambda sc, d=dst_ap, b=bank_i, f=f, p=p:
                     sc.copy(out=d, in_=self.bank(b, 0, p)[0:f, :]), sem)
        self.bset(bank_i, sem)
        return sem.n

    def build(self):
        nc = self.nc
        st = self.stage
        ins, wait, ld, ldT = self.ins, self.wait, self.ld, self.ldT

        def di(name, shape, dt=F32):
            return nc.dram_tensor(name, shape, dt, kind="ExternalInput")
        X = di("x", [S, D])
        Wq = di("wq", [D, D]); Bq = di("bq", [D])
        Wk = di("wk", [D, D]); Bk = di("bk", [D])
        Wv = di("wv", [D, D]); Bv = di("bv", [D])
        Wo = di("wo", [D, D]); Bo = di("bo", [D])
        Wup = di("wup", [D_CROSS, D]); Bup = di("bup", [D_CROSS])
        Wdn = di("wdn", [D_PRIV, D_CROSS]); Bdn = di("bdn", [D_PRIV])
        Wqr = di("wqr", [1024, D_PRIV])
        KEYS = di("keys", [8, 128, 2, 64])
        DE = di("de", [N_EXPERTS, D_PRIV])
        UE = di("ue", [N_EXPERTS, D])
        EYEf = di("eyef", [128, 128])
        EYEb = di("eyeb", [128, 128], BF16)
        CAUS_KQ = di("caus_kq", [128, 128])
        CAUS_QK = di("caus_qk", [128, 128])
        ONESf = di("onesf", [1, 128])
        FOLDC = di("foldc", [8, 128, 32])
        NEGS = di("negs", [128, 1])
        ONESb = di("onesb", [1, 128], BF16)
        ONESc = di("onesc", [128, 1], BF16)
        OUT = nc.dram_tensor("out", [S, D], F32, kind="ExternalOutput")
        WupB = nc.dram_tensor("wupb", [D_CROSS, D], BF16)
        WdnB = nc.dram_tensor("wdnb", [D_PRIV, D_CROSS], BF16)
        WqrB = nc.dram_tensor("wqrb", [1024, D_PRIV], BF16)
        WoB = nc.dram_tensor("wob", [D, D], BF16)
        TBL = nc.dram_tensor("tbl", [N_EXPERTS, TROW], BF16)
        OALL = nc.dram_tensor("oall", [nT, NH, 128, 128], BF16)
        HTM = nc.dram_tensor("htm_s", [nT, 128, D_PRIV], BF16)
        ATTN = nc.dram_tensor("attn_s", [nT, 128, D], BF16)

        self.dbg = {}
        def dout(name, shape, dt=F32):
            if name in self.dbgnames:
                self.dbg[name] = nc.dram_tensor(
                    "dbg_" + name, shape, dt, kind="ExternalOutput")
            return self.dbg.get(name)
        D_XN = dout("xn", [nDC, 128, S], BF16)
        D_H = dout("h", [nPC, 128, S], BF16)
        D_HTM = dout("htm", [nT, 128, D_PRIV], BF16)
        D_Q = dout("q", [64, 16, 128], BF16)
        D_SIM = dout("sim", [nT, 128, 16, 128])
        D_S1V = dout("s1v", [nT, 128, 2, 8, 8])
        D_S1F = dout("s1f", [nT, 128, 2, 8, 8])
        D_ESEL = dout("esel", [128, nT, 64])
        D_GATES = dout("gates", [128, nT, 64])
        D_Z = dout("z", [nT, 128, 64])
        D_FFN = dout("ffn", [nT, 128, D])
        D_QH = dout("qh", [NH, 128, S], BF16)
        D_KH = dout("kh", [NH, 128, S], BF16)
        D_PT = dout("pt", [128, S], BF16)
        D_SUMS = dout("sums", [NH, 128, nT])
        D_ATT = dout("att", [nT, 128, D], BF16)

        def asb(name, shape, dt):
            return nc.alloc_sbuf_tensor(name, shape, dt)

        def aat(name, shape, dt, off):
            return nc.alloc_sbuf_tensor_at(name, shape, dt, offset=off)

        eyef = asb("eyef_s", [128, 128], F32)
        eyeb = asb("eyeb_s", [128, 128], BF16)
        caus_kq = asb("caus_kq_s", [128, 128], F32)
        caus_qk = asb("caus_qk_s", [128, 128], F32)
        onesf = asb("onesf_s", [1, 128], F32)
        foldc = asb("foldc_s", [128, 8, 32], F32)
        negs = asb("negs_s", [128, 1], F32)
        onesb = asb("onesb_s", [1, 128], BF16)
        onesc = asb("onesc_s", [128, 1], BF16)
        bias_up = asb("bias_up", [128, nCC], F32)
        bias_dn = asb("bias_dn", [128, nPC], F32)
        bias_q = asb("bias_q", [128, nDC], F32)
        bias_k = asb("bias_k", [128, nDC], F32)
        bv_sb = asb("bv_sb", [128, nDC], F32)
        bv_bb = asb("bv_bb", [128, nDC], BF16)
        stat = asb("stat", [128, 8], F32)
        stat2 = asb("stat2", [128, 16], F32)
        bias_stage = asb("bias_stage", [64, 128], F32)
        bstg_dn = asb("bstg_dn", [8, 128], F32)
        bstg_q = asb("bstg_q", [16, 128], F32)
        bstg_k = asb("bstg_k", [16, 128], F32)
        bstg_v = asb("bstg_v", [16, 128], F32)
        trash = asb("trash", [128, 1024], BF16)
        trash_b = asb("trash_b", [128, 1024], BF16)
        keysT = asb("keysT", [64, 16, 128], BF16)
        s1v = asb("s1v", [128, 2, 8, 8], F32)
        s1i = asb("s1i", [128, 2, 8, 8], U32)
        s1f = asb("s1f", [128, 2, 8, 8], F32)
        alls = asb("alls", [128, 8, 64], F32)
        scv = asb("scv", [128, 8, 8], F32)
        gexp = asb("gexp", [128, 8, 8], F32)
        sum8 = asb("sum8", [128, 8], F32)
        pk32 = asb("pk32", [128, 8, 8], U32)
        ku32 = asb("ku32", [128, 8, 8], U32)
        kxf = asb("kxf", [128, 8, 8], F32)
        kyf = asb("kyf", [128, 8, 8], F32)
        sel_t = asb("sel_t", [128, 8, 8], F32)
        sel_x = asb("sel_x", [128, 8, 8], F32)
        sel_y = asb("sel_y", [128, 8, 8], F32)
        msk8 = asb("msk8", [128, 8, 8], F32)
        esel = asb("esel", [128, nT, 64], F32)
        gates = asb("gates", [128, nT, 64], F32)
        sums_h = asb("sums_h", [128, nT], F32)
        sumsp = asb("sumsp", [128, 8], F32)
        recip_h = asb("recip_h", [128, nT], F32)
        idx_t = [asb(f"idx_t{i}", [128, 512], I16) for i in range(2)]
        zcol = asb("zcol", [128, 64], F32)
        zw = asb("zw", [128, 64], F32)
        gcol = asb("gcol", [128, 64], F32)

        ln_mean = asb("ln_mean", [128, 1], F32)
        ln_var = asb("ln_var", [128, 1], F32)
        ln_rstd = asb("ln_rstd", [128, 1], F32)
        atl = [asb(f"atl{i}", [128, 512], BF16) for i in range(2)]

        TOPB = 229120
        R_A = TOPB - 65536
        R_B = R_A - 65536
        R_C = R_B - 45056

        def overlay(name, shape, dt, off):
            return aat(name, shape, dt, off)

        xn = overlay("xn", [128, nDC, S], BF16, R_A)
        gbuf = [overlay(f"gbuf{i}", [128, SLOTC, TROW], BF16,
                        R_A + i * (SLOTC * TROW * 2)) for i in range(NGB)]
        h_f32 = overlay("h_f32", [128, nPC, S], F32, R_B)
        h_fm = overlay("h_fm", [128, nPC, S], BF16, R_B)
        h_tm = overlay("h_tm", [128, nT, D_PRIV], BF16, R_B + 32768)
        qh = overlay("qh", [128, S], BF16, R_B)
        kh = overlay("kh", [128, S], BF16, R_B + 4096)
        vfm = overlay("vfm", [128, S], F32, R_B + 8192)
        vtm = overlay("vtm", [128, nDC, 128], BF16, R_B + 16384)
        ptil = overlay("ptil", [128, S], BF16, R_B + 20480)
        wsl_pp = [overlay(f"wsl_pp{i}", [128, nDC, 128], BF16,
                          R_B + 24576 + i * 4096) for i in range(2)]
        sdiag = [overlay(f"sdiag{i}", [128, 128], BF16,
                         R_B + 16384 + i * 256) for i in range(8)]
        bo_row = overlay("bo_row", [1, D], F32, R_B)
        bo_rowb = overlay("bo_rowb", [1, D], BF16, R_B + 8192)
        bop_row = overlay("bop_row", [1, D], BF16, R_C + 40960)
        ztrash = [overlay(f"ztrash{i}", [128, 1024], BF16,
                          R_B + 24576 + i * 2048) for i in range(4)]
        wo_tile = overlay("wo_tile", [128, NH, 512], BF16, R_B)
        otl1 = overlay("otl1", [128, NH, 128], BF16, R_B + 18432)
        x_t = [overlay(f"x_t{i}", [128, D], F32, R_C + i * 8192)
               for i in range(2)]
        xn_tm = [overlay("xn_tm0", [128, D], F32, R_C + 16384),
                 overlay("xn_tm1", [128, D], F32, R_B)]
        wup_g = [overlay(f"wup_g{i}", [128, nDC, CG * 128], BF16,
                         R_C + i * 16384) for i in range(2)]
        wdn_g = overlay("wdn_g", [128, CG, D_PRIV], BF16, R_C + 32768)
        h1g = overlay("h1g", [128, CG, 512], BF16, R_C + 40960)
        wqr_t = overlay("wqr_t", [128, 16 * nPC, 64], BF16, R_C)
        q64 = overlay("q64", [64, 16, 128], BF16, R_C + 16384)
        sim_vb = overlay("sim_vb", [128, 16, 128], F32, R_C + 20480)
        keys_raw = overlay("keys_raw", [128, 16, 64], F32, R_C + 28672)
        htm_stage = [overlay(f"htm_stage{i}", [128, 1024], BF16,
                             R_C + 28672 + i * 2048) for i in range(2)]
        rrow = overlay("rrow", [1, 512], F32, R_C + 32768)
        rb_sb = overlay("rb_sb", [128, 512], F32, R_C + 34816)
        otile = [overlay(f"otile{i}", [128, 512], BF16,
                         R_C + 36864 + i * 1024) for i in range(2)]
        wo_cols = overlay("wo_cols", [128, NH, 512], BF16, R_C)
        otl = [overlay(f"otl{i}", [128, NH, 128], BF16, R_C + 16384 + i * 4096)
               for i in range(2)]
        att_t = [overlay(f"att_t{i}", [128, D], BF16, R_C + i * 4096)
                 for i in range(2)]
        xin_t = [overlay(f"xin_t{i}", [128, D], F32, R_C + 8192 + i * 8192)
                 for i in range(2)]
        out_t = [overlay(f"out_t{i}", [128, D], F32, R_C + 24576 + i * 8192)
                 for i in range(2)]

        guard = asb("alloc_guard", [128, 4], F32)
        gend = nc.lookup_mloc(guard).addr + 16
        assert gend <= R_C, f"smalls overflow into R_C: {gend} > {R_C}"

        self.psum = nc.alloc_psum_tensor("psum", [128, 4096], F32)
        self.bank_free = {i: (None, 0) for i in range(8)}

        self.s_x = s_x = self.mksem("s_x")
        self.s_mm = s_mm = self.mksem("s_mm")
        self.s_dve = s_dve = self.mksem("s_dve")
        self.s_act = s_act = self.mksem("s_act")
        self.s_out = s_out = self.mksem("s_out")
        s_outb = self.mksem("s_outb")
        s_xin = [self.mksem("s_xin0"), self.mksem("s_xin1")]
        s_cwup = self.mksem("s_cwup")
        s_cwdn = self.mksem("s_cwdn")
        s_cwqr = self.mksem("s_cwqr")
        s_cwo = self.mksem("s_cwo")
        s_ctbl = self.mksem("s_ctbl")
        s_wup = self.mksem("s_wup")
        s_wdn = self.mksem("s_wdn")
        s_wqr = self.mksem("s_wqr")
        s_wv2 = [self.mksem("s_wv0"), self.mksem("s_wv1")]
        s_wo = self.mksem("s_wo")
        s_otl = [self.mksem("s_otl0"), self.mksem("s_otl1")]
        s_ost = [self.mksem("s_ost0"), self.mksem("s_ost1")]
        s_ast = [self.mksem("s_ast0"), self.mksem("s_ast1")]
        s_gat2 = [self.mksem(f"s_gat{i}") for i in range(NGB)]
        s_asm = [self.mksem("s_asm0"), self.mksem("s_asm1")]
        s_idxL = [self.mksem("s_idxL0"), self.mksem("s_idxL1")]
        s_dbg = self.mksem("s_dbg")
        outsems = [s_out, s_outb]

        self._gregs = {}
        def _init_gp(g):
            g.load_library(MLP_LIB)
            self._gregs["nidx"] = g.to_reg(GCALL)
        ins("gpsimd", _init_gp)
        ld(eyef[:], EYEf[:])
        ld(eyeb[:], EYEb[:])
        ld(caus_kq[:], CAUS_KQ[:])
        ld(caus_qk[:], CAUS_QK[:])
        ld(onesf[:], ONESf[:])
        ld(foldc[:], FOLDC[:].rearrange("u t p -> t u p"))
        ld(negs[:], NEGS[:])
        ld(onesb[:], ONESb[:])
        ld(onesc[:], ONESc[:])
        bstage_v = ld(bias_stage[0:64, :],
                      Bup[:].rearrange("(c p) -> c p", p=128))
        ld(bstg_dn[:], Bdn[:].rearrange("(c p) -> c p", p=128))
        ld(bstg_q[:], Bq[:].rearrange("(c p) -> c p", p=128))
        ld(bstg_k[:], Bk[:].rearrange("(c p) -> c p", p=128))
        ld(bstg_v[:], Bv[:].rearrange("(c p) -> c p", p=128))
        consts_v = 0
        for pp in range(2):
            for hh in range(8):
                consts_v = ld(keys_raw[:, pp * 8 + hh, :], KEYS[hh, :, pp, :])

        def cast_dma(dst, src, pieces, sm):
            n = src.shape[0]
            stp = n // pieces
            for i in range(pieces):
                ins("gpsimd", lambda g, d=dst[i * stp:(i + 1) * stp],
                    s=src[i * stp:(i + 1) * stp]: g.dma_start(out=d, in_=s),
                    sm, 16)
            return sm.n

        wup_cast_v = cast_dma(WupB, Wup, 8, s_cwup)
        wdn_cast_v = cast_dma(WdnB, Wdn, 4, s_cwdn)
        wqr_cast_v = cast_dma(WqrB, Wqr, 1, s_cwqr)
        wo_cast_v = cast_dma(WoB, Wo, 2, s_cwo) if st >= 4 else 0

        wait("tensor", s_x, consts_v)
        self.transpose_evac(bias_stage[0:64, :], bias_up[:], 6, eyef[:])
        self.transpose_evac(bstg_dn[:], bias_dn[:], 7, eyef[:])
        self.transpose_evac(bstg_q[:], bias_q[:], 6, eyef[:])
        self.transpose_evac(bstg_k[:], bias_k[:], 7, eyef[:])
        self.transpose_evac(bstg_v[:], bv_sb[:], 6, eyef[:])

        x_free_d = [0, 0]
        x_free_a = [0, 0]
        xn_tm_free = [0, 0]
        for T in range(nT):
            bi = T % 2
            b = 8 * bi
            wait("sync", s_dve, x_free_d[bi])
            wait("sync", s_act, x_free_a[bi])
            xv = ld(x_t[bi][:], X[128 * T:128 * T + 128, :], sem=s_xin[bi])
            xt = x_t[bi]
            wait("vector", s_xin[bi], xv)
            ins("vector", lambda v, xt=xt, b=b: v.reduce_sum(
                stat2[:, b + 0:b + 1], xt[:], axis=AX.X), s_dve)
            sum_v = s_dve.n
            wait("scalar", s_xin[bi], xv)
            ins("scalar", lambda sc, xt=xt, b=b: sc.activation(
                out=trash[:, 0:1024], in_=xt[:, 0:1024], func=AF.Square,
                accum_out=stat2[:, b + 1:b + 2]), s_act)
            ins("scalar", lambda sc, xt=xt, b=b: sc.activation(
                out=trash_b[:, 0:1024], in_=xt[:, 1024:2048],
                func=AF.Square,
                accum_out=stat2[:, b + 2:b + 3]), s_act)
            act_sq_v = s_act.n
            x_free_a[bi] = act_sq_v
            wait("vector", s_dve, sum_v)
            wait("vector", s_act, act_sq_v)
            ins("vector", lambda v, b=b: v.tensor_scalar(
                out=stat2[:, b + 3:b + 4], in0=stat2[:, b + 0:b + 1],
                scalar1=1.0 / D, scalar2=None, op0=ALU.mult))
            ins("vector", lambda v, b=b: v.tensor_tensor(
                out=stat2[:, b + 4:b + 5], in0=stat2[:, b + 1:b + 2],
                in1=stat2[:, b + 2:b + 3], op=ALU.add), s_dve)
            wait("vector", s_dve, s_dve.n)
            ins("vector", lambda v, b=b: v.tensor_tensor(
                out=stat2[:, b + 5:b + 6], in0=stat2[:, b + 3:b + 4],
                in1=stat2[:, b + 3:b + 4], op=ALU.mult), s_dve)
            wait("vector", s_dve, s_dve.n)
            ins("vector", lambda v, b=b: v.scalar_tensor_tensor(
                out=stat2[:, b + 6:b + 7], in0=stat2[:, b + 4:b + 5],
                scalar=1.0 / D, in1=stat2[:, b + 5:b + 6],
                op0=ALU.mult, op1=ALU.subtract), s_dve)
            wait("vector", s_dve, s_dve.n)
            ins("vector", lambda v, b=b: v.tensor_scalar(
                out=stat2[:, b + 6:b + 7], in0=stat2[:, b + 6:b + 7],
                scalar1=1e-5, scalar2=None, op0=ALU.add), s_dve)
            wait("vector", s_dve, s_dve.n)
            ins("vector", lambda v, b=b: v.reciprocal(
                out=stat2[:, b + 6:b + 7], in_=stat2[:, b + 6:b + 7]), s_dve)
            varpe_v = s_dve.n
            wait("scalar", s_dve, varpe_v)
            ins("scalar", lambda sc, b=b: sc.activation(
                out=stat2[:, b + 7:b + 8], in_=stat2[:, b + 6:b + 7],
                func=AF.Sqrt), s_act)
            rstd_v = s_act.n
            wait("vector", s_act, rstd_v)
            ins("vector", lambda v, b=b: v.tensor_tensor(
                out=stat2[:, b + 5:b + 6], in0=stat2[:, b + 3:b + 4],
                in1=stat2[:, b + 7:b + 8], op=ALU.mult), s_dve)
            wait("vector", s_dve, s_dve.n)
            wait("vector", s_mm, xn_tm_free[bi])
            ins("vector", lambda v, xt=xt, bi=bi, b=b: v.tensor_scalar(
                out=xn_tm[bi][:], in0=xt[:], scalar1=stat2[:, b + 7:b + 8],
                scalar2=stat2[:, b + 5:b + 6],
                op0=ALU.mult, op1=ALU.subtract), s_dve)
            big_v = s_dve.n
            x_free_d[bi] = big_v
            wait("tensor", s_dve, big_v)
            for j in range(nDC):
                self.transpose_evac(
                    xn_tm[bi][:, 128 * j:128 * j + 128],
                    xn[:, j, 128 * T:128 * T + 128], 6 + (j % 2), eyef[:],
                    eng="vector" if j % 2 == 0 else "scalar")
            xn_tm_free[bi] = s_mm.n
        xn_done_dve = s_dve.n
        xn_done_act = s_act.n

        if D_XN is not None:
            wait("scalar", s_dve, xn_done_dve)
            wait("scalar", s_act, xn_done_act)
            for j in range(nDC):
                self.st(D_XN[j], xn[:, j, :], sem=s_dbg)

        if st < 2:
            self.epilogue(OUT, out_t, write_dummy=True,
                          extra_sems=(s_dbg,))
            return

        for sl in range(16):
            self.transpose_evac(keys_raw[:, sl, :], keysT[:, sl, :], 6,
                                eyef[:])
        keysT_v = s_dve.n

        wup_busy = [0, 0]
        wdn_busy_v = 0
        hacc_evac_v = 0
        h1_evac = {6: self.bank_free[6], 7: self.bank_free[7]}
        wait("sync", s_dve, keysT_v)
        wait("sync", s_act, xn_done_act)
        wait("tensor", s_dve, xn_done_dve)
        wait("tensor", s_act, xn_done_act)
        s_wup2 = [self.mksem("s_wup0"), self.mksem("s_wup1")]
        wup_seen = [0, 0]
        for cg in range(nCC // CG):
            gbn = cg % 2
            wait("sync", s_mm, wup_busy[gbn])
            wait("sync", s_cwup,
                 16 * ((128 * CG * (cg + 1) - 1) // (D_CROSS // 8) + 1))
            if cg == 0:
                wait("sync", s_cwdn, wdn_cast_v)
            wup_v = wdn_v = 0
            for d in range(nDC):
                wup_v = ldT(wup_g[gbn][:, d, :],
                            WupB[128 * CG * cg:128 * CG * (cg + 1),
                                 128 * d:128 * d + 128], sem=s_wup2[gbn])
            wait("sync", s_mm, wup_busy[1 - gbn])
            for ci in range(CG):
                c = cg * CG + ci
                wdn_v = ldT(wdn_g[:, ci, :],
                            WdnB[0:D_PRIV, 128 * c:128 * c + 128], sem=s_wdn)
            for sp in range(4):
                tok0 = 512 * sp
                wait("tensor", s_wup2[gbn], wup_v)
                wait("tensor", s_dve, hacc_evac_v)
                for ci in range(CG):
                    bk_i = 6 + (ci % 2)
                    self.waitt("tensor", h1_evac[bk_i])
                    for d in range(nDC):
                        ins("tensor", lambda t, ci=ci, d=d, b=bk_i, tok0=tok0,
                            gbn=gbn: t.matmul(
                                self.bank(b),
                                wup_g[gbn][:, d, 128 * ci:128 * ci + 128],
                                xn[:, d, tok0:tok0 + 512],
                                start=(d == 0), stop=(d == nDC - 1)),
                            s_mm if d == nDC - 1 else None)
                    mmv = s_mm.n
                    wait("scalar", s_mm, mmv)
                    wait("scalar", s_mm, wdn_busy_v)
                    ins("scalar", lambda sc, ci=ci, b=bk_i, c=cg * CG + ci:
                        sc.activation(out=h1g[:, ci, :], in_=self.bank(b),
                                      func=AF.Silu,
                                      bias=bias_up[:, c:c + 1]), s_act)
                    h1_evac[bk_i] = (s_act, s_act.n)
                h1gv = s_act.n
                wait("tensor", s_act, h1gv)
                wait("tensor", s_wdn, wdn_v)
                wait("tensor", s_dve, hacc_evac_v)
                for pc in range(nPC):
                    for ci in range(CG):
                        ins("tensor", lambda t, pc=pc, ci=ci: t.matmul(
                            self.bank(pc),
                            wdn_g[:, ci, 128 * pc:128 * pc + 128],
                            h1g[:, ci, :],
                            start=(ci == 0), stop=(ci == CG - 1)),
                            s_mm if ci == CG - 1 else None)
                wdn_busy_v = s_mm.n
                mmv = s_mm.n
                wait("vector", s_mm, mmv)
                for pc in range(nPC):
                    if cg == 0:
                        ins("vector", lambda v, pc=pc, tok0=tok0:
                            v.tensor_copy(
                                out=h_f32[:, pc, tok0:tok0 + 512],
                                in_=self.bank(pc)), s_dve)
                    else:
                        ins("vector", lambda v, pc=pc, tok0=tok0:
                            v.tensor_tensor(
                                out=h_f32[:, pc, tok0:tok0 + 512],
                                in0=h_f32[:, pc, tok0:tok0 + 512],
                                in1=self.bank(pc), op=ALU.add), s_dve)
                hacc_evac_v = s_dve.n
            wup_busy[gbn] = s_mm.n
        if st >= 3:
            wait("gpsimd", s_wup2[0], s_wup2[0].n)
            wait("gpsimd", s_wup2[1], s_wup2[1].n)
            wait("gpsimd", s_wdn, s_wdn.n)
            cast_dma(TBL[:, 0:D_PRIV], DE, 8, s_ctbl)
            cast_dma(TBL[:, D_PRIV:TROW], UE, 16, s_ctbl)
        tbl_cast_v = s_ctbl.n
        for i in range(8):
            self.bank_free[i] = (s_dve, hacc_evac_v)
        self.bank_free[6] = h1_evac[6]
        self.bank_free[7] = h1_evac[7]
        wait("vector", s_dve, s_dve.n)
        for pc in range(nPC):
            ins("vector", lambda v, pc=pc: v.tensor_scalar(
                out=h_f32[:, pc, :], in0=h_f32[:, pc, :],
                scalar1=bias_dn[:, pc:pc + 1], scalar2=None, op0=ALU.add),
                s_dve)
        hbias_v = s_dve.n
        wait("tensor", s_dve, hbias_v)
        hst_free = [(None, 0), (None, 0)]
        s_htm = self.mksem("s_htm")
        s_htr = [self.mksem("s_htr0"), self.mksem("s_htr1")]
        for T in range(nT):
            r = T % 2
            self.waitt("vector", hst_free[r])
            for pc in range(nPC):
                self.transpose_evac(h_f32[:, pc, 128 * T:128 * T + 128],
                                    htm_stage[r][:, 128 * pc:128 * pc + 128],
                                    6 + (pc % 2), eyef[:],
                                    eng="vector" if pc % 2 == 0 else "scalar")
            hv = s_dve.n
            wait("scalar", s_dve, hv)
            wait("scalar", s_act, s_act.n)
            sv = self.st(HTM[T], htm_stage[r][:], sem=s_htr[r])
            hst_free[r] = (s_htr[r], sv)
        htm_stored_v = [s_htr[0].n, s_htr[1].n]
        htr_mm_v = s_mm.n
        wait("vector", s_mm, htr_mm_v)
        ins("vector", lambda v: v.tensor_copy(
            out=trash[:, 0:1024], in_=h_f32[:, 0, 0:1024]))
        ins("vector", lambda v: v.tensor_copy(
            out=trash_b[:, 0:1024], in_=h_f32[:, 0, 1024:2048]), s_dve)
        wait("vector", s_dve, s_dve.n)
        ins("vector", lambda v: v.tensor_copy(
            out=h_fm[:, 0, 0:1024], in_=trash[:, 0:1024]))
        ins("vector", lambda v: v.tensor_copy(
            out=h_fm[:, 0, 1024:2048], in_=trash_b[:, 0:1024]), s_dve)
        wait("vector", s_dve, s_dve.n)
        for pc in range(1, nPC):
            ins("vector", lambda v, pc=pc: v.tensor_copy(
                out=h_fm[:, pc, :], in_=h_f32[:, pc, :]), s_dve)
            wait("vector", s_dve, s_dve.n)
        h_fm_v = s_dve.n
        wait("sync", s_htr[0], htm_stored_v[0])
        wait("sync", s_htr[1], htm_stored_v[1])
        wait("sync", s_dve, h_fm_v)
        htm_ld_v = ld(h_tm[:], HTM[:].rearrange("t p x -> p t x"),
                      sem=s_htm)
        h_tm_v = None
        if D_H is not None:
            wait("scalar", s_dve, h_fm_v)
            for pc in range(nPC):
                self.st(D_H[pc], h_fm[:, pc, :], sem=s_dbg)
        if D_HTM is not None:
            wait("scalar", s_htm, htm_ld_v)
            for T in range(nT):
                self.st(D_HTM[T], h_tm[:, T, :], sem=s_dbg)

        wait("sync", s_cwqr, wqr_cast_v)
        wait("sync", s_mm, max(wup_busy))
        wqr_v = 0
        for sl in range(16):
            for pc in range(nPC):
                wqr_v = ldT(wqr_t[:, sl * nPC + pc, :],
                            WqrB[64 * sl:64 * sl + 64,
                                 128 * pc:128 * pc + 128], sem=s_wqr)

        def insv(fn):
            ins("vector", fn, s_dve)
            wait("vector", s_dve, s_dve.n)

        routing_done_v = 0
        dbgq_v = 0
        for T in range(nT):
            tok0 = 128 * T
            wait("tensor", s_wqr, wqr_v)
            if dbgq_v:
                wait("vector", s_dbg, s_dbg.n)
            for sl in range(16):
                bk_i = 6 + (sl % 2)
                self.bwait("tensor", bk_i)
                for pc in range(nPC):
                    ins("tensor", lambda t, sl=sl, pc=pc, b=bk_i, tok0=tok0:
                        t.matmul(self.bank(b, 0, 128)[0:64, :],
                                 wqr_t[:, sl * nPC + pc, :],
                                 h_fm[:, pc, tok0:tok0 + 128],
                                 start=(pc == 0), stop=(pc == nPC - 1)),
                        s_mm if pc == nPC - 1 else None)
                mmv = s_mm.n
                wait("vector", s_mm, mmv)
                ins("vector", lambda v, sl=sl, b=bk_i: v.tensor_copy(
                    out=q64[:, sl, :], in_=self.bank(b, 0, 128)[0:64, :]),
                    s_dve)
                self.bset(bk_i, s_dve)
            q64_v = s_dve.n
            if D_Q is not None and T == 0:
                wait("scalar", s_dve, q64_v)
                self.st(D_Q[:], q64[:], sem=s_dbg)
                dbgq_v = s_dbg.n
            wait("tensor", s_dve, max(q64_v, keysT_v))
            for sl in range(16):
                bk_i = 6 + (sl % 2)
                self.bwait("tensor", bk_i)
                ins("tensor", lambda t, sl=sl, b=bk_i: t.matmul(
                    self.bank(b, 0, 128), q64[:, sl, :], keysT[:, sl, :],
                    start=True, stop=True), s_mm)
                mmv = s_mm.n
                wait("vector", s_mm, mmv)
                ins("vector", lambda v, sl=sl, b=bk_i: v.tensor_copy(
                    out=sim_vb[:, sl, :], in_=self.bank(b, 0, 128)), s_dve)
                self.bset(bk_i, s_dve)
            sim_v = s_dve.n
            if D_SIM is not None:
                wait("scalar", s_dve, sim_v)
                self.st(D_SIM[T], sim_vb[:], sem=s_dbg)
            wait("vector", s_dve, s_dve.n)
            for pp in range(2):
                for h in range(8):
                    ins("vector", lambda v, pp=pp, h=h: v.max(
                        out=s1v[:, pp, h, :], in_=sim_vb[:, pp * 8 + h, :]),
                        s_dve)
            wait("vector", s_dve, s_dve.n)
            for pp in range(2):
                for h in range(8):
                    ins("vector", lambda v, pp=pp, h=h: v.max_index(
                        out=s1i[:, pp, h, :], in_max=s1v[:, pp, h, :],
                        in_values=sim_vb[:, pp * 8 + h, :]), s_dve)
            wait("vector", s_dve, s_dve.n)
            insv(lambda v: v.tensor_copy(out=s1f[:], in_=s1i[:]))
            if D_S1V is not None:
                ins("vector", lambda v: v.engine_nop(), s_dve)
                wait("scalar", s_dve, s_dve.n)
                self.st(D_S1V[T], s1v[:], sem=s_dbg)
                self.st(D_S1F[T], s1f[:], sem=s_dbg)
            for h in range(8):
                ins("vector", lambda v, h=h: v.tensor_tensor(
                    out=alls[:, h, :].rearrange("p (a b) -> p a b", a=8),
                    in0=s1v[:, 0, h, :].rearrange("p (a o) -> p a o", o=1
                        ).to_broadcast([128, 8, 8]),
                    in1=s1v[:, 1, h, :].rearrange("p (o b) -> p o b", o=1
                        ).to_broadcast([128, 8, 8]),
                    op=ALU.add), s_dve)
            wait("vector", s_dve, s_dve.n)
            for h in range(8):
                ins("vector", lambda v, h=h: v.max(
                    out=scv[:, h, :], in_=alls[:, h, :]), s_dve)
            wait("vector", s_dve, s_dve.n)
            for h in range(8):
                ins("vector", lambda v, h=h: v.max_index(
                    out=pk32[:, h, :], in_max=scv[:, h, :],
                    in_values=alls[:, h, :]), s_dve)
            wait("vector", s_dve, s_dve.n)
            insv(lambda v: v.tensor_single_scalar(
                out=ku32[:], in_=pk32[:], scalar=3,
                op=ALU.logical_shift_right))
            insv(lambda v: v.tensor_copy(out=kxf[:], in_=ku32[:]))
            insv(lambda v: v.tensor_single_scalar(
                out=ku32[:], in_=pk32[:], scalar=7, op=ALU.bitwise_and))
            insv(lambda v: v.tensor_copy(out=kyf[:], in_=ku32[:]))
            for a in range(8):
                insv(lambda v, a=a: v.tensor_single_scalar(
                    out=msk8[:], in_=kxf[:], scalar=float(a),
                    op=ALU.is_equal))
                insv(lambda v, a=a: v.tensor_tensor(
                    out=sel_t[:], in0=msk8[:],
                    in1=s1f[:, 0, :, a].rearrange("p (h o) -> p h o", o=1
                        ).to_broadcast([128, 8, 8]),
                    op=ALU.mult))
                if a == 0:
                    insv(lambda v: v.tensor_copy(
                        out=sel_x[:], in_=sel_t[:]))
                else:
                    insv(lambda v: v.tensor_tensor(
                        out=sel_x[:], in0=sel_x[:], in1=sel_t[:], op=ALU.add))
                insv(lambda v, a=a: v.tensor_single_scalar(
                    out=msk8[:], in_=kyf[:], scalar=float(a),
                    op=ALU.is_equal))
                insv(lambda v, a=a: v.tensor_tensor(
                    out=sel_t[:], in0=msk8[:],
                    in1=s1f[:, 1, :, a].rearrange("p (h o) -> p h o", o=1
                        ).to_broadcast([128, 8, 8]),
                    op=ALU.mult))
                if a == 0:
                    insv(lambda v: v.tensor_copy(
                        out=sel_y[:], in_=sel_t[:]))
                else:
                    insv(lambda v: v.tensor_tensor(
                        out=sel_y[:], in0=sel_y[:], in1=sel_t[:], op=ALU.add))
            ins("vector", lambda v, T=T: v.scalar_tensor_tensor(
                out=esel[:, T, :].rearrange("p (h j) -> p h j", h=8),
                in0=sel_x[:], scalar=128.0, in1=sel_y[:],
                op0=ALU.mult, op1=ALU.add), s_dve)
            esel_v = s_dve.n
            wait("scalar", s_dve, esel_v)
            ins("scalar", lambda sc: sc.activation(
                out=gexp[:], in_=scv[:], func=AF.Exp), s_act)
            wait("vector", s_act, s_act.n)
            for h in range(8):
                ins("vector", lambda v, h=h: v.reduce_sum(
                    sum8[:, h:h + 1], gexp[:, h, :], axis=AX.X),
                    s_dve if h == 7 else None)
            wait("vector", s_dve, s_dve.n)
            ins("vector", lambda v: v.reciprocal(out=sum8[:], in_=sum8[:]),
                s_dve)
            wait("vector", s_dve, s_dve.n)
            ins("vector", lambda v, T=T: v.tensor_tensor(
                out=gates[:, T, :].rearrange("p (h j) -> p h j", h=8),
                in0=gexp[:],
                in1=sum8[:].rearrange("p (h o) -> p h o", o=1
                    ).to_broadcast([128, 8, 8]),
                op=ALU.mult), s_dve)
            routing_done_v = s_dve.n

        if D_ESEL is not None:
            wait("scalar", s_dve, routing_done_v)
            self.st(D_ESEL[:], esel[:], sem=s_dbg)
        if D_GATES is not None:
            wait("scalar", s_dve, routing_done_v)
            self.st(D_GATES[:], gates[:], sem=s_dbg)

        h_fm_last_use = s_mm.n
        xn_last_use = s_mm.n
        p35_psum_free = (None, 0)
        attn_done_v = 0

        if st >= 4:
            wsl_busy = [(None, 0), (None, 0)]
            wsl_load_v = [0, 0]
            qkv_banks_free = (None, 0)
            wait("gpsimd", s_mm, h_fm_last_use)
            projs = [(h, pj, W) for h in range(NH)
                     for pj, W in (("q", Wq), ("k", Wk), ("v", Wv))]

            def proj_load(i):
                h2, pj2, Wmat2 = projs[i]
                pb = i % 2
                self.waitt("gpsimd", wsl_busy[pb])
                wait("gpsimd", s_wv2[pb], wsl_load_v[pb])
                for d in range(nDC):
                    self.ins("gpsimd", lambda g, d=d, pb=pb, h2=h2,
                             Wmat2=Wmat2: g.dma_start(
                                 out=wsl_pp[pb][:, d, :],
                                 in_=Wmat2[128 * d:128 * d + 128,
                                           128 * h2:128 * h2 + 128]),
                             s_wv2[pb], 16)
                wsl_load_v[pb] = s_wv2[pb].n

            proj_load(0)
            for h in range(NH):
                for pjn, (proj, Wmat) in enumerate(
                        (("q", Wq), ("k", Wk), ("v", Wv))):
                    i = 3 * h + pjn
                    pb = i % 2
                    if i + 1 < len(projs):
                        proj_load(i + 1)
                    self.waitt("tensor", qkv_banks_free)
                    wait("tensor", s_wv2[pb], wsl_load_v[pb])
                    for d in range(nDC):
                        for sg in range(4):
                            ins("tensor", lambda t, d=d, sg=sg, pb=pb:
                                t.matmul(
                                    self.bank(sg), wsl_pp[pb][:, d, :],
                                    xn[:, d, 512 * sg:512 * sg + 512],
                                    start=(d == 0), stop=(d == nDC - 1)),
                                s_mm if (d == nDC - 1 and sg == 3) else None)
                    wsl_busy[pb] = (s_mm, s_mm.n)
                    xn_last_use = s_mm.n
                    mmv = s_mm.n
                    wait("scalar", s_mm, mmv)
                    if proj == "q":
                        ins("scalar", lambda sc, h=h: sc.activation(
                            out=qh[:], in_=self.psum[:, 0:2048],
                            func=AF.Identity, bias=bias_q[:, h:h + 1]),
                            s_act)
                    elif proj == "k":
                        ins("scalar", lambda sc, h=h: sc.activation(
                            out=kh[:], in_=self.psum[:, 0:2048],
                            func=AF.Identity, bias=bias_k[:, h:h + 1]),
                            s_act)
                    else:
                        ins("scalar", lambda sc: sc.copy(
                            out=vfm[:], in_=self.psum[:, 0:2048]), s_act)
                    qkv_banks_free = (s_act, s_act.n)
                qkv_v = s_act.n
                if D_QH is not None:
                    wait("scalar", s_act, qkv_v)
                    self.st(D_QH[h], qh[:], sem=s_dbg)
                    self.st(D_KH[h], kh[:], sem=s_dbg)
                wait("tensor", s_act, qkv_v)
                for kc in range(nDC):
                    self.transpose_evac(vfm[:, 128 * kc:128 * kc + 128],
                                        vtm[:, kc, :], 6 + (kc % 2), eyef[:])
                vtm_v = s_dve.n
                self.waitt("tensor", qkv_banks_free)
                for sg2 in range(4):
                    sb_i = 6 + (sg2 % 2)
                    self.bwait("tensor", sb_i)
                    self.bwait("tensor", sg2)
                    nkc = 4 * sg2 + 4

                    def a2(kc, sg2=sg2):
                        off = max(0, 128 * kc - 512 * sg2)
                        w = 512 - off
                        bk_i = 4 + (kc % 2)
                        self.bwait("tensor", bk_i)
                        ins("tensor", lambda t, kc=kc, sg2=sg2, off=off, w=w,
                            b=bk_i: t.matmul(
                                self.bank(b, off, w),
                                kh[:, 128 * kc:128 * kc + 128],
                                qh[:, 512 * sg2 + off:512 * sg2 + off + w],
                                start=True, stop=True), s_mm)
                        mmv = s_mm.n
                        if kc >= 4 * sg2:
                            wait("vector", s_mm, mmv)
                            ins("vector", lambda v, b=bk_i, off=off:
                                v.tensor_tensor(
                                    out=self.bank(b, off, 128),
                                    in0=self.bank(b, off, 128),
                                    in1=caus_kq[:], op=ALU.add), s_dve)
                            wait("scalar", s_dve, s_dve.n)
                        else:
                            wait("scalar", s_mm, mmv)
                        ins("scalar", lambda sc, b=bk_i, off=off, w=w,
                            kc=kc: sc.activation(
                                out=ptil[:, 512 * (kc % 2) + off:
                                         512 * (kc % 2) + off + w],
                                in_=self.bank(b, off, w), func=AF.Exp,
                                scale=INV_SQRT_HD, bias=negs[:]), s_act)
                        self.bset(bk_i, s_act)
                        return s_act.n

                    ptv = a2(0)
                    for kc in range(nkc):
                        off = max(0, 128 * kc - 512 * sg2)
                        w = 512 - off
                        nxt = a2(kc + 1) if kc + 1 < nkc else None
                        wait("tensor", s_act, ptv)
                        if kc == 0 and sg2 == 0:
                            wait("tensor", s_dve, vtm_v)
                        last = (kc == nkc - 1)
                        ins("tensor", lambda t, kc=kc, sg2=sg2, off=off, w=w,
                            last=last: t.matmul(
                                self.bank(sg2, off, w), vtm[:, kc, :],
                                ptil[:, 512 * (kc % 2) + off:
                                     512 * (kc % 2) + off + w],
                                start=(kc == 0), stop=last),
                            s_mm if last else None)
                        ins("tensor", lambda t, kc=kc, off=off, w=w,
                            b=sb_i, last=last: t.matmul(
                                self.bank(b, off, w)[0:1, :], onesc[:],
                                ptil[:, 512 * (kc % 2) + off:
                                     512 * (kc % 2) + off + w],
                                start=(kc == 0), stop=last),
                            s_mm if last else None)
                        ptv = nxt
                    av_done = s_mm.n
                    wait("vector", s_mm, av_done)
                    ins("vector", lambda v, b=sb_i: v.tensor_copy(
                        out=rrow[:], in_=self.bank(b, 0, 512)[0:1, :]),
                        s_dve)
                    wait("vector", s_dve, s_dve.n)
                    ins("vector", lambda v: v.reciprocal(
                        out=rrow[:], in_=rrow[:]), s_dve)
                    rrv = s_dve.n
                    self.bset(sb_i, s_dve)
                    self.bwait("tensor", sb_i)
                    wait("tensor", s_dve, rrv)
                    ins("tensor", lambda t, b=sb_i: t.matmul(
                        self.bank(b), onesf[:], rrow[:],
                        start=True, stop=True), s_mm)
                    mmv = s_mm.n
                    wait("vector", s_mm, mmv)
                    ins("vector", lambda v, b=sb_i: v.tensor_copy(
                        out=rb_sb[:], in_=self.bank(b)), s_dve)
                    self.bset(sb_i, s_dve)
                    wait("vector", s_dve, s_dve.n)
                    ot = otile[sg2 % 2]
                    if hasattr(self, "ot_store"):
                        wait("vector", s_ost[sg2 % 2],
                             self.ot_store[sg2 % 2])
                    ins("vector", lambda v, sg2=sg2, ot=ot:
                        v.scalar_tensor_tensor(
                            out=ot[:],
                            in0=self.bank(sg2, 0, 512),
                            scalar=1.0, in1=rb_sb[:],
                            op0=ALU.bypass, op1=ALU.mult), s_dve)
                    self.bset(sg2, s_dve)
                    ov = s_dve.n
                    wait("sync", s_dve, ov)
                    sv = self.st(
                        OALL[4 * sg2:4 * sg2 + 4, h].rearrange(
                            "t p x -> p t x"),
                        ot[:].rearrange("p (t x) -> p t x", t=4),
                        sem=s_ost[sg2 % 2], eng="sync")
                    if not hasattr(self, "ot_store"):
                        self.ot_store = [0, 0]
                    self.ot_store[sg2 % 2] = sv
                qkv_banks_free = (s_dve, s_dve.n)

            wait("sync", s_cwo, wo_cast_v)
            wait("sync", s_act, s_act.n)
            wait("sync", s_mm, s_mm.n)
            s_bol = self.mksem("s_bol")
            bol_v = ld(bo_row[:], Bo[:].rearrange("(o d) -> o d", o=1),
                       sem=s_bol)
            wait("vector", s_bol, bol_v)
            ins("vector", lambda v: v.tensor_copy(
                out=bo_rowb[:], in_=bo_row[:]), s_dve)
            ins("vector", lambda v: v.tensor_copy(
                out=bv_bb[:], in_=bv_sb[:]), s_dve)
            borow_v = s_dve.n
            wo_v = 0
            wait("sync", s_mm, s_mm.n)
            wo_first_v = None
            p35_bank_free = [(s_act, s_act.n), (s_act, s_act.n)]
            att_sv = [0, 0]
            oall_done_v = [s_ost[0].n, s_ost[1].n]
            for dsg in range(4):
                wait("sync", s_mm, s_mm.n if dsg > 0 else 0)
                for hc in range(NH):
                    wo_v = ld(wo_cols[:, hc, :],
                              WoB[128 * hc:128 * hc + 128,
                                  512 * dsg:512 * dsg + 512], sem=s_wo)
                self.bwait("tensor", 6)
                wait("tensor", s_wo, wo_v)
                if dsg == 0:
                    wait("tensor", s_dve, borow_v)
                for hc in range(NH):
                    ins("tensor", lambda t, hc=hc: t.matmul(
                        self.bank(6)[0:1, :], bv_bb[:, hc:hc + 1],
                        wo_cols[:, hc, :], start=(hc == 0), stop=False), None)
                ins("tensor", lambda t, dsg=dsg: t.matmul(
                    self.bank(6)[0:1, :], onesb[0:1, 0:1],
                    bo_rowb[:, 512 * dsg:512 * dsg + 512],
                    start=False, stop=True), s_mm)
                mmv = s_mm.n
                wait("vector", s_mm, mmv)
                ins("vector", lambda v, dsg=dsg: v.tensor_copy(
                    out=bop_row[:, 512 * dsg:512 * dsg + 512],
                    in_=self.bank(6)[0:1, :]), s_dve)
                self.bset(6, s_dve)
                bop_v = s_dve.n
            wait("sync", s_ost[0], oall_done_v[0])
            wait("sync", s_ost[1], oall_done_v[1])
            p35_psum_free = (s_dve, s_dve.n)

        if st < 3:
            self.epilogue(OUT, out_t, write_dummy=True,
                          extra_sems=(s_dbg,))
            return

        self._dbgh = dict(TBL=TBL, idx_t=idx_t, gbuf=gbuf)
        wait("gpsimd", s_mm, xn_last_use)
        wait("gpsimd", s_ctbl, tbl_cast_v)
        wait("gpsimd", s_cwup, s_cwup.n)
        wait("gpsimd", s_cwdn, s_cwdn.n)
        wait("gpsimd", s_cwqr, s_cwqr.n)
        if st >= 4:
            wait("gpsimd", s_cwo, s_cwo.n)
        wait("vector", s_htm, htm_ld_v)
        ins("vector", lambda v: v.memset(idx_t[0][:], 0))
        ins("vector", lambda v: v.memset(idx_t[1][:], 0), s_dve)
        wait("vector", s_dve, s_dve.n)
        wait("sync", s_mm, s_mm.n)
        wait("sync", s_dve, s_dve.n)
        wait("sync", s_act, s_act.n)
        wait("sync", s_dbg, s_dbg.n)
        self.waitt("tensor", p35_psum_free)
        idx_free = [(None, 0), (None, 0)]
        gbuf_free = [(None, 0) for _ in range(NGB)]
        ffn_banks_v = 0
        asm_free = [(None, 0), (None, 0)]
        last_gather_v = [0] * NGB
        self.sdguard = [0, 0, 0, 0]
        gcall = 0
        self.wo_busy_v = 0
        otl_busy_v = 0
        for T in range(nT):
            tok0 = 128 * T
            ir = T % 2
            bg = 4
            wait("tensor", s_dve, ffn_banks_v)
            self.bwait("tensor", 2)
            self.waitt("vector", idx_free[ir])
            wait("tensor", s_dve, routing_done_v)
            for u in range(8):
                ins("tensor", lambda t, u=u, T=T: t.matmul(
                    self.bank(2, 64 * u, 64)[0:32, :], foldc[:, u, :],
                    esel[:, T, :], start=True, stop=True),
                    s_mm if u == 7 else None)
            foldv = s_mm.n
            wait("vector", s_mm, foldv)
            ins("vector", lambda v, ir=ir: v.tensor_copy(
                out=idx_t[ir][0:32, :].rearrange(
                    "p (k u) -> p u k", u=8),
                in_=self.bank(2)[0:32, :].rearrange(
                    "p (u k) -> p u k", k=64)), s_dve)
            self.bset(2, s_dve)
            idxv = s_dve.n
            wait("gpsimd", s_dve, idxv)
            wait("tensor", s_dve, idxv)
            wait("sync", s_mm, otl_busy_v)
            olv = ld(otl1[:], OALL[T].rearrange("h p x -> p h x"),
                     sem=s_otl[0])

            def attn_dsg(dsg, T=T, ir=ir, olv=olv):
                nonlocal_state = None
                ab = dsg % 2
                self.wait("sync", s_mm, self.wo_busy_v)
                wv = 0
                for hc in range(NH):
                    wv = self.ld(wo_tile[:, hc, :],
                                 WoB[128 * hc:128 * hc + 128,
                                     512 * dsg:512 * dsg + 512], sem=s_wo)
                self.bwait("tensor", ab)
                wait("tensor", s_wo, wv)
                if dsg == 0:
                    wait("tensor", s_otl[0], olv)
                    if T == 0:
                        wait("tensor", s_dve, bop_v)
                for hc in range(NH):
                    ins("tensor", lambda t, hc=hc, ab=ab: t.matmul(
                        self.bank(ab), otl1[:, hc, :],
                        wo_tile[:, hc, :], start=(hc == 0), stop=False),
                        None)
                ins("tensor", lambda t, dsg=dsg, ab=ab: t.matmul(
                    self.bank(ab), onesb[:],
                    bop_row[:, 512 * dsg:512 * dsg + 512],
                    start=False, stop=True), s_mm)
                mmv = s_mm.n
                self.wo_busy_v = mmv
                wait("vector", s_mm, mmv)
                ins("vector", lambda v, ab=ab, ir=ir, dsg=dsg: v.tensor_copy(
                    out=att_t[ir][:, 512 * dsg:512 * dsg + 512],
                    in_=self.bank(ab)), s_dve)
                self.bset(ab, s_dve)
                return mmv
            cinfo = {}
            IC = GCALL // 16

            def consume(cc, bg=bg):
                gbc, gvc, siluv, gci = cinfo.pop(cc)
                sd0 = 2 * (gci % 4)
                wait("vector", s_act, siluv)
                wait("vector", s_mm, self.sdguard[gci % 4])
                for j in range(SLOTC):
                    slot = SLOTC * cc + j
                    ins("vector", lambda v, j=j, slot=slot, sd0=sd0:
                        v.tensor_scalar(
                            out=sdiag[sd0 + j][:], in0=eyeb[:],
                            scalar1=gcol[:, slot:slot + 1], scalar2=None,
                            op0=ALU.mult), s_dve)
                sdv = s_dve.n
                wait("tensor", s_dve, sdv)
                wait("tensor", s_gat2[gbc], gvc)
                for j in range(SLOTC):
                    slot = SLOTC * cc + j
                    for sg in range(4):
                        ins("tensor", lambda t, gbc=gbc, j=j, sg=sg, bg=bg,
                            slot=slot, sd0=sd0: t.matmul(
                                self.bank(bg + sg), sdiag[sd0 + j][:],
                                gbuf[gbc][:, j, D_PRIV + 512 * sg:
                                          D_PRIV + 512 * sg + 512],
                                start=(slot == 0), stop=(slot == 63)),
                            s_mm if (sg == 3 and j == SLOTC - 1) else None)
                gbuf_free[gbc] = (s_mm, s_mm.n)
                self.sdguard[gci % 4] = s_mm.n

            for c in range(nGC):
                gb = gcall % NGB
                self.waitt("gpsimd", gbuf_free[gb])
                wait("gpsimd", s_gat2[gb], last_gather_v[gb])
                ins("gpsimd", lambda g, gb=gb, c=c, ir=ir: g.dma_gather(
                    out_ap=gbuf[gb][:], in_ap=TBL[:],
                    idxs_ap=idx_t[ir][:, IC * c:IC * c + IC],
                    num_idxs=GCALL, num_idxs_reg=self._gregs["nidx"],
                    elem_size=TROW),
                    s_gat2[gb], 16)
                gv = s_gat2[gb].n
                last_gather_v[gb] = gv
                wait("vector", s_gat2[gb], gv)
                for j in range(SLOTC):
                    slot = SLOTC * c + j
                    ins("vector", lambda v, gb=gb, j=j, slot=slot, T=T:
                        v.scalar_tensor_tensor(
                            out=ztrash[(SLOTC * c + j) % 4][:, 0:1024],
                            in0=gbuf[gb][:, j, 0:D_PRIV], scalar=1.0,
                            in1=h_tm[:, T, :], op0=ALU.bypass, op1=ALU.mult,
                            accum_out=zcol[:, slot:slot + 1]),
                        s_dve if j == SLOTC - 1 else None)
                wait("vector", s_dve, s_dve.n)
                ins("vector", lambda v, c=c, T=T: v.tensor_tensor(
                    out=zw[:, SLOTC * c:SLOTC * c + SLOTC],
                    in0=zcol[:, SLOTC * c:SLOTC * c + SLOTC],
                    in1=gates[:, T, SLOTC * c:SLOTC * c + SLOTC],
                    op=ALU.mult), s_dve)
                zv = s_dve.n
                wait("scalar", s_dve, zv)
                ins("scalar", lambda sc, c=c: sc.activation(
                    out=gcol[:, SLOTC * c:SLOTC * c + SLOTC],
                    in_=zw[:, SLOTC * c:SLOTC * c + SLOTC], func=AF.Silu),
                    s_act)
                cinfo[c] = (gb, gv, s_act.n, gcall)
                if c >= 1:
                    consume(c - 1)
                if c % 7 == 5 and c // 7 < 4:
                    mmv = attn_dsg(c // 7)
                    if c // 7 == 3:
                        otl_busy_v = mmv
                gcall += 1
            consume(nGC - 1)
            idx_free[ir] = (s_gat2[gb], s_gat2[gb].n)
            ffn_mm_v = s_mm.n
            att_fill_v = s_dve.n
            if D_ATT is not None:
                wait("scalar", s_dve, att_fill_v)
                self.st(D_ATT[T], att_t[ir][:], sem=s_dbg)
            if D_Z is not None:
                ins("vector", lambda v: v.engine_nop(), s_dve)
                wait("scalar", s_dve, s_dve.n)
                self.st(D_Z[T], zcol[:], sem=s_dbg)
            self.waitt("sync", asm_free[ir])
            xv = ld(xin_t[ir][:], X[tok0:tok0 + 128, :], sem=s_asm[ir])
            wait("vector", s_mm, ffn_mm_v)
            wait("vector", s_asm[ir], s_asm[ir].n)
            if D_FFN is not None:
                ins("vector", lambda v, bg=bg, ir=ir: v.tensor_copy(
                    out=out_t[ir][:],
                    in_=self.psum[:, 512 * bg:512 * bg + 2048]), s_dve)
                wait("scalar", s_dve, s_dve.n)
                self.st(D_FFN[T], out_t[ir][:], sem=s_dbg)
                wait("vector", s_dbg, s_dbg.n)
            ins("vector", lambda v, ir=ir, bg=bg: v.tensor_tensor(
                out=out_t[ir][:],
                in0=self.psum[:, 512 * bg:512 * bg + 2048],
                in1=xin_t[ir][:], op=ALU.add), s_dve)
            ffn_banks_v = s_dve.n
            if st >= 4:
                wait("vector", s_dve, s_dve.n)
                ins("vector", lambda v, ir=ir: v.tensor_tensor(
                    out=out_t[ir][:], in0=out_t[ir][:], in1=att_t[ir][:],
                    op=ALU.add), s_dve)
            ov = s_dve.n
            ins("vector", lambda v: v.engine_nop(), s_dve)
            wait("scalar", s_dve, s_dve.n)
            sv = self.st(OUT[tok0:tok0 + 128, :], out_t[ir][:],
                         sem=outsems[ir])
            asm_free[ir] = (outsems[ir], sv)
        self.epilogue(OUT, out_t, write_dummy=False)

    def epilogue(self, OUT, out_t, write_dummy, extra_sems=()):
        ins, wait = self.ins, self.wait
        if write_dummy:
            for sm in extra_sems:
                wait("vector", sm, sm.n)
            ins("vector", lambda v: v.memset(out_t[0][:], 0.0), self.s_dve)
            wait("scalar", self.s_dve, self.s_dve.n)
            for T in range(nT):
                self.st(OUT[128 * T:128 * T + 128, :], out_t[0][:])
                wait("scalar", self.s_out, self.s_out.n)
        for sm in (self.s_out,) + tuple(extra_sems):
            wait("scalar", sm, sm.n)
            wait("sync", sm, sm.n)
            wait("gpsimd", sm, sm.n)

    def emit(self):
        nc = self.nc
        with nc.Block() as block:
            @block.sync
            def _(eng):
                for fn in self.ops["sync"]:
                    fn(eng)

            @block.gpsimd
            def _(eng):
                for fn in self.ops["gpsimd"]:
                    fn(eng)

            @block.tensor
            def _(eng):
                for fn in self.ops["tensor"]:
                    fn(eng)

            @block.vector
            def _(eng):
                for fn in self.ops["vector"]:
                    fn(eng)

            @block.scalar
            def _(eng):
                for fn in self.ops["scalar"]:
                    fn(eng)
        mybir.codegen_inst_isa_subclasses(nc)
        return nc


def build(stage=5, debug_outputs=()):
    mk = MK(stage=stage, debug_outputs=debug_outputs)
    return mk.emit(), mk


from concourse.bass_utils import run_bass_kernel_spmd


def kernel(**inputs):
    def a32(k):
        v = np.asarray(inputs[k])
        assert v.dtype == np.float32, (k, v.dtype)
        return np.ascontiguousarray(v)

    x = a32("L0_attn_x")
    consts = host_consts()
    base = {
        "wq": a32("L0_attn_Wq"), "bq": a32("L0_attn_bq"),
        "wk": a32("L0_attn_Wk"), "bk": a32("L0_attn_bk"),
        "wv": a32("L0_attn_Wv"), "bv": a32("L0_attn_bv"),
        "wo": a32("L0_attn_Wo"), "bo": a32("L0_attn_bo"),
        "wup": a32("L0_ffn_W_up"), "bup": a32("L0_ffn_b_up"),
        "wdn": a32("L0_ffn_W_down"), "bdn": a32("L0_ffn_b_down"),
        "wqr": a32("L0_ffn_W_q"), "keys": a32("L0_ffn_keys"),
        "de": a32("L0_ffn_down_embed"), "ue": a32("L0_ffn_up_embed"),
        **consts,
    }
    nc, _mk = build(stage=5)
    in_maps = [dict(base, x=x[c]) for c in range(8)]
    res = run_bass_kernel_spmd(nc, in_maps, core_ids=list(range(8)))
    out = np.stack([np.asarray(res.results[c]["out"]) for c in range(8)])
    return out.astype(np.float32)
